# version 12
# baseline (speedup 1.0000x reference)
import contextlib
import numpy as np
import ml_dtypes
import concourse.bass as bass
import concourse.mybir as mybir
from concourse.bass_utils import run_bass_kernel_spmd

F32 = mybir.dt.float32
BF16 = mybir.dt.bfloat16
I32 = mybir.dt.int32
AF = mybir.ActivationFunctionType
ALU = mybir.AluOpType
AX = mybir.AxisListType

D = 1024
H = 16
HD = 64
FF = 3072
NHC = 24
NDC = 8
EPS_LN = 1e-5
EPS_ATT = 1e-6
EPS_L2 = 1e-12
SCALE_ATT = float(64 ** -0.5)
DEQ = 1.0 / 127.0

ENGS = ("sync", "vector", "scalar", "tensor", "gpsimd")


class Edge:
    def __init__(self, es, nc, name):
        self.h = es.enter_context(nc.semaphore(name))
        self.n = 0


class Plan:
    def __init__(self):
        self.q = {e: [] for e in ENGS}
        self.lastw = {e: {} for e in ENGS}

    def op(self, eng, deps, inc, build, k=1):
        dd = []
        for ed, v in deps:
            if ed is None or v is None or v <= 0:
                continue
            key = id(ed)
            if self.lastw[eng].get(key, 0) >= v:
                continue
            self.lastw[eng][key] = v
            dd.append((ed.h, v))
        icap = None
        if inc is not None:
            inc.n += k
            icap = (inc.h, k)

        def thunk(eo, dd=tuple(dd), icap=icap, build=build):
            for h, v in dd:
                eo.wait_ge(h, v)
            ins = build(eo)
            if icap is not None:
                ins.then_inc(icap[0], icap[1])

        self.q[eng].append(thunk)
        return inc.n if inc is not None else None


def build_program(TCORE=4096, detect_races=False):
    NT = TCORE // 128
    TG = 512
    NTG = TCORE // TG
    NTT = TG // 128

    nc = bass.Bass("TRN2", target_bir_lowering=False, debug=False,
                   detect_race_conditions=detect_races)

    x_in = nc.dram_tensor("x", [TCORE, D], F32, kind="ExternalInput").ap()
    wqkv = nc.dram_tensor("wqkv", [D, 3 * D], F32, kind="ExternalInput").ap()
    wg_i = nc.dram_tensor("wg", [NHC, D, 128], I32, kind="ExternalInput").ap()
    wu_i = nc.dram_tensor("wu", [NHC, D, 128], I32, kind="ExternalInput").ap()
    wd_i = nc.dram_tensor("wd", [FF, D], I32, kind="ExternalInput").ap()
    ident_in = nc.dram_tensor("ident", [128, 128], BF16, kind="ExternalInput").ap()
    y_out = nc.dram_tensor("y", [TCORE, D], F32, kind="ExternalOutput").ap()

    wg_bf = nc.dram_tensor("wg_bf", [NHC, 128, 1024], BF16).ap()
    wu_bf = nc.dram_tensor("wu_bf", [NHC, 128, 1024], BF16).ap()
    qT_scr = nc.dram_tensor("qT_scr", [NT, 128, 1024], BF16).ap()
    xnT_scr = nc.dram_tensor("xnT_scr", [NDC, NT, 128, 128], BF16).ap()
    cc_in = nc.dram_tensor("cc_in", [128, 520], F32)
    cc_out = nc.dram_tensor("cc_out", [128, 520], F32)

    es = contextlib.ExitStack()
    with es:
        def sb(name, shape, dt):
            return es.enter_context(nc.sbuf_tensor(name, shape, dt))

        xb4 = sb("xb4", [128, 4 * D], F32)
        xb = [xb4.ap()[:, i * D:(i + 1) * D] for i in range(4)]
        qn = sb("qn", [128, D], F32)
        kn = sb("kn", [128, D], F32)
        eab = sb("eab", [128, D], BF16)
        ep2b = sb("ep2b", [128, D], F32)
        outb = [sb("outb0", [128, D], F32), sb("outb1", [128, D], F32)]
        sq_k = sb("sq_k", [128, D], BF16)
        xnb = [sb("xnb0", [128, D], BF16), sb("xnb1", [128, D], BF16)]
        qhat = [sb("qhat0", [128, D], BF16), sb("qhat1", [128, D], BF16)]
        khat = [sb("khat0", [128, D], BF16), sb("khat1", [128, D], BF16)]
        vtile = [sb("vt0", [128, H * 65], BF16), sb("vt1", [128, H * 65], BF16)]
        xnT_t = [sb("xnTt0", [128, D], BF16), sb("xnTt1", [128, D], BF16),
                 sb("xnTt2", [128, D], BF16), sb("xnTt3", [128, D], BF16)]
        qh_st = [sb("qhst0", [128, D], BF16), sb("qhst1", [128, D], BF16)]
        wcast = [sb("wc0", [128, 1024], BF16), sb("wc1", [128, 1024], BF16)]
        xnT_g = [sb("xnTg0", [128, NDC * TG], BF16), sb("xnTg1", [128, NDC * TG], BF16)]
        _xbt = xb4.ap()[:, 2 * D:4 * D].bitcast(BF16)
        wg_sb = [_xbt[:, 0:1024], _xbt[:, 1024:2048]]
        wu_sb = [_xbt[:, 2048:3072], _xbt[:, 3072:4096]]
        wd_sb = sb("wd_sb", [128, NHC * 1024], BF16)
        bigW = sb("bigW", [128, NDC * 3 * D], BF16)
        kv_bd = sb("kv_bd", [128, 8 * 130], BF16)
        ident = sb("ident_sb", [128, 128], BF16)
        wi = [sb("wi0", [128, 1024], I32), sb("wi1", [128, 1024], I32)]
        qT_t = [wi[0].ap().bitcast(BF16)[:, 0:1024], wi[1].ap().bitcast(BF16)[:, 0:1024]]
        cst = sb("cst", [128, 3], F32)
        ssum = sb("ssum", [128, 2], F32)
        ssq = sb("ssq", [128, 2], F32)
        mb = sb("mb", [128, 2], F32)
        mm2 = sb("mm2", [128, 2], F32)
        varb = sb("varb", [128, 2], F32)
        stdv = sb("stdv", [128, 2], F32)
        rstd = sb("rstd", [128, 2], F32)
        negmr = sb("negmr", [128, 2], F32)
        ssq_q = sb("ssq_q", [128, H], F32)
        ssq_k = sb("ssq_k", [128, H], F32)
        nrm2_q = sb("nrm2_q", [128, H], F32)
        nrm2_k = sb("nrm2_k", [128, H], F32)
        rst_q = sb("rst_q", [128, H], F32)
        rst_k = sb("rst_k", [128, H], F32)
        rden = [sb("rden0", [128, H], F32), sb("rden1", [128, H], F32)]

        sqdump = outb[0].ap().bitcast(BF16)[:, 0:D]
        cc_sb = outb[1].ap()[:, 0:520]
        sg = [kn.ap()[:, 0:512], kn.ap()[:, 512:1024]]
        ep_scr1 = qn.ap()[:, 0:D]
        ep_scr2 = ep2b.ap()[:]
        P_rA = es.enter_context(nc.psum_tensor("P_rA", [128, 1024], F32))
        P_rB = es.enter_context(nc.psum_tensor("P_rB", [128, 1024], F32))
        P_kv = es.enter_context(nc.psum_tensor("P_kv", [128, 1024], F32))
        P_m1 = es.enter_context(nc.psum_tensor("P_m1", [128, 512], F32))
        P_m2 = es.enter_context(nc.psum_tensor("P_m2", [128, 512], F32))
        P_m1b = P_m1.ap().bitcast(BF16)
        P_m2b = P_m2.ap().bitcast(BF16)

        E_s = Edge(es, nc, "E_s")
        E_sd = Edge(es, nc, "E_sd")
        E_gd = Edge(es, nc, "E_gd")
        E_t = Edge(es, nc, "E_t")
        E_v = Edge(es, nc, "E_v")
        E_sc = Edge(es, nc, "E_sc")
        E_cc = Edge(es, nc, "E_cc")
        E_g = Edge(es, nc, "E_g")

        P = Plan()
        op = P.op

        s_x = {}; sc_sq = {}; v_var = {}; sc_sqrt = {}; v_stats = {}; sc_xn = {}
        t_trx = {}; v_evacX = {}; gd_xnt = {}
        t_q = {}; sc_qsq = {}; v_gq = {}; sc_nq = {}; v_qmul = {}; sc_expq = {}; sc_qhat = {}
        t_k = {}; sc_ksq = {}; v_gk = {}; sc_nk = {}; v_kmul = {}; sc_expk = {}; sc_khat = {}
        t_v = {}; v_vev = {}; t_kv = {}; t_trq = {}; v_evacQ = {}; gd_qt = {}
        gd_wg = {}; sc_cwg = {}; gd_swg = {}
        gd_wu = {}; v_cwu = {}; gd_swu = {}
        gd_wd = {}; sc_cwd = {}
        t_gate = {}; sc_silu = {}; t_up = {}; v_hmul = {}
        s_w = {}; s_xg = {}; s_x3 = {}; sd_qt = {}
        t_down = {}; t_attn = {}
        v_ep0 = {}; v_ep1 = {}; v_ep2 = {}; v_ep3 = {}; gd_out = {}

        op("sync", [], E_s, lambda e: e.dma_start(out=ident[:], in_=ident_in), k=16)
        s_ident = E_s.n
        op("vector", [], E_v, lambda e: e.memset(vtile[0][:], 1.0))
        op("vector", [], E_v, lambda e: e.memset(vtile[1][:], 1.0))
        op("vector", [], E_v, lambda e: e.memset(kv_bd[:], 0.0))
        op("vector", [], E_v, lambda e: e.memset(cst[:, 0:1], EPS_LN))
        op("vector", [], E_v, lambda e: e.memset(cst[:, 1:2], 1.0))
        op("vector", [], E_v, lambda e: e.memset(cst[:, 2:3], EPS_L2 * EPS_L2))
        v_init = E_v.n

        wq_cast = {}
        for kk in range(NDC * 3):
            dc, s = divmod(kk, 3)
            par = kk % 2
            deps = [(E_sd, None)]
            if kk >= 2:
                pe, pv = wq_cast[kk - 2]
                deps = [((E_sc if pe == "scalar" else E_v), pv)]
            op("scalar", deps, E_sd,
               lambda e, dc=dc, s=s, par=par: e.dma_start(
                   out=outb[par][:], in_=wqkv[dc * 128:(dc + 1) * 128, s * 1024:(s + 1) * 1024]), k=16)
            ld = E_sd.n
            eng = "scalar" if (kk % 2 == 0) else "vector"
            ed = E_sc if eng == "scalar" else E_v
            col = dc * 3 * D + s * 1024
            if eng == "scalar":
                op(eng, [(E_sd, ld)], ed,
                   lambda e, par=par, col=col: e.activation(
                       bigW[:, col:col + 1024], outb[par][:], AF.Copy))
            else:
                op(eng, [(E_sd, ld)], ed,
                   lambda e, par=par, col=col: e.tensor_copy(bigW[:, col:col + 1024], outb[par][:]))
            wq_cast[kk] = (eng, ed.n)
        sc_wq = E_sc.n
        v_wq = E_v.n

        def conv_sched(t):
            return range(t * NHC // NT, (t + 1) * NHC // NT)

        sc_qcopy = {}
        v_kcopy = {}
        v_xn = {}
        sc_evacQ = {}
        sc_qsoft = {}
        v_qhat = {}
        sc_khat = {}
        v_gq = {}
        v_gk = {}
        NP = NT // 2

        def front_x(t):
            op("sync", [(E_v, v_xn.get(t - 4)), (E_sc, sc_sq.get(t - 4))], E_s,
               lambda e, t=t: e.dma_start(out=xb[t % 4], in_=x_in[t * 128:(t + 1) * 128, :]), k=16)
            s_x[t] = E_s.n

        def round_mm(t, base, Pr, deps):
            tpar = t % 4
            for dc in range(NDC):
                for hf in range(2):
                    ed = E_t if (dc == NDC - 1 and hf == 1) else None
                    op("tensor", deps if (dc == 0 and hf == 0) else [], ed,
                       lambda e, dc=dc, hf=hf, tpar=tpar, Pr=Pr, base=base: e.matmul(
                           Pr.ap()[:, hf * 512:(hf + 1) * 512],
                           xnT_t[tpar][:, dc * 128:(dc + 1) * 128],
                           bigW[:, dc * 3 * D + base + hf * 512: dc * 3 * D + base + (hf + 1) * 512],
                           start=(dc == 0), stop=(dc == NDC - 1)))
            return E_t.n

        def kv_col(p):
            return (p // 4) * 512 + (p % 4) * 65

        def backT_kv(t):
            deps = [(E_sc, sc_khat[t]), (E_v, v_vev[t])]
            first = True
            for p in range(8):
                for eh in range(2):
                    h = 2 * p + eh
                    st = (t == 0) and (p % 4 == 0)
                    sp = (t == NT - 1) and (p % 4 == 3)
                    last = (p == 7) and (eh == 1)
                    op("tensor", deps if first else [], E_t if last else None,
                       lambda e, p=p, eh=eh, h=h, st=st, sp=sp, t=t: e.matmul(
                           P_kv.ap()[eh * 64:(eh + 1) * 64, kv_col(p):kv_col(p) + 65],
                           khat[t % 2][:, h * 64:(h + 1) * 64],
                           vtile[t % 2][:, h * 65:(h + 1) * 65],
                           start=st, stop=sp, skip_group_check=True))
                    first = False
            t_kv[t] = E_t.n

        def conv(tp):
            for hc in range(tp * NHC // NP, (tp + 1) * NHC // NP):
                op("gpsimd", [(E_sc, sc_cwd.get(hc - 1))], E_gd,
                   lambda e, hc=hc: e.dma_start(
                       out=wi[0][:].rearrange("p (dc c) -> p dc c", dc=NDC),
                       in_=wg_i[hc].rearrange("(dc p) c -> p dc c", p=128)), k=16)
                gd_wg[hc] = E_gd.n
                op("scalar", [(E_gd, E_gd.n), (E_gd, gd_swg.get(hc - 1))], E_sc,
                   lambda e: e.activation(wcast[0][:], wi[0][:], AF.Copy, scale=DEQ))
                sc_cwg[hc] = E_sc.n
                op("gpsimd", [(E_sc, E_sc.n)], E_gd,
                   lambda e, hc=hc: e.dma_start(out=wg_bf[hc], in_=wcast[0][:]), k=16)
                gd_swg[hc] = E_gd.n
                op("gpsimd", [(E_v, v_cwu.get(hc - 1))], E_gd,
                   lambda e, hc=hc: e.dma_start(
                       out=wi[1][:].rearrange("p (dc c) -> p dc c", dc=NDC),
                       in_=wu_i[hc].rearrange("(dc p) c -> p dc c", p=128)), k=16)
                gd_wu[hc] = E_gd.n
                op("vector", [(E_gd, E_gd.n), (E_gd, gd_swu.get(hc - 1))], E_v,
                   lambda e: e.tensor_scalar_mul(wcast[1][:], wi[1][:], DEQ))
                v_cwu[hc] = E_v.n
                op("gpsimd", [(E_v, E_v.n)], E_gd,
                   lambda e, hc=hc: e.dma_start(out=wu_bf[hc], in_=wcast[1][:]), k=16)
                gd_swu[hc] = E_gd.n
                op("gpsimd", [(E_sc, sc_cwg[hc])], E_gd,
                   lambda e, hc=hc: e.dma_start(out=wi[0][:], in_=wd_i[hc * 128:(hc + 1) * 128, :]), k=16)
                op("scalar", [(E_gd, E_gd.n)], E_sc,
                   lambda e, hc=hc: e.activation(wd_sb[:, hc * 1024:(hc + 1) * 1024], wi[0][:],
                                                 AF.Copy, scale=DEQ))
                sc_cwd[hc] = E_sc.n
                g_conv[hc] = sc_cwd[hc]

        g_conv = {}

        def ladder(t, which):
            par = t % 2
            Pr = ((P_rA if t % 2 == 0 else P_rB) if which == 0
                  else (P_rB if t % 2 == 0 else P_rA))
            if which == 0:
                op("scalar", [(E_t, t_q[t]), (E_v, v_qhat.get(t - 1))], E_sc,
                   lambda e, Pr=Pr: e.copy(qn[:], Pr.ap()[:]))
                sc_qcopy[t] = E_sc.n
                op("scalar", [(E_sc, E_sc.n), (E_v, v_gq.get(t - 1))], E_sc,
                   lambda e: e.activation(eab[:], qn[:], AF.Square))
                op("vector", [(E_sc, E_sc.n)], E_v,
                   lambda e: e.reduce_sum(
                       ssq_q[:], eab[:].rearrange("p (h d) -> p h d", h=H), axis=AX.X))
                v_gq[t] = E_v.n
                op("scalar", [(E_v, E_v.n)], E_sc,
                   lambda e: e.activation(nrm2_q[:], ssq_q[:], AF.Ln, bias=cst[:, 2:3]))
                op("scalar", [(E_sc, E_sc.n)], E_sc,
                   lambda e: e.activation(rst_q[:], nrm2_q[:], AF.Exp, scale=-0.5))
                op("vector", [(E_sc, E_sc.n), (E_t, t_trq.get(t - 2))], E_v,
                   lambda e, par=par: e.tensor_tensor(
                       qhat[par][:].rearrange("p (h d) -> p h d", h=H),
                       qn[:].rearrange("p (h d) -> p h d", h=H),
                       rst_q[:].unsqueeze(2).broadcast_to([128, H, HD]),
                       ALU.mult))
                v_qhat[t] = E_v.n
            else:
                op("vector", [(E_t, t_k[t]), (E_v, v_kmul.get(t - 1))], E_v,
                   lambda e, Pr=Pr: e.tensor_copy(kn[:], Pr.ap()[:]))
                v_kcopy[t] = E_v.n
                op("scalar", [(E_v, E_v.n), (E_v, v_gk.get(t - 1))], E_sc,
                   lambda e: e.activation(sq_k[:], kn[:], AF.Square))
                op("vector", [(E_sc, E_sc.n)], E_v,
                   lambda e: e.reduce_sum(
                       ssq_k[:], sq_k[:].rearrange("p (h d) -> p h d", h=H), axis=AX.X))
                v_gk[t] = E_v.n
                op("scalar", [(E_v, E_v.n)], E_sc,
                   lambda e: e.activation(nrm2_k[:], ssq_k[:], AF.Ln, bias=cst[:, 2:3]))
                op("scalar", [(E_sc, E_sc.n)], E_sc,
                   lambda e: e.activation(rst_k[:], nrm2_k[:], AF.Exp, scale=-0.5))
                op("vector", [(E_sc, E_sc.n)], E_v,
                   lambda e: e.tensor_tensor(
                       eab[:].rearrange("p (h d) -> p h d", h=H),
                       kn[:].rearrange("p (h d) -> p h d", h=H),
                       rst_k[:].unsqueeze(2).broadcast_to([128, H, HD]),
                       ALU.mult))
                v_kmul[t] = E_v.n
                op("scalar", [(E_v, E_v.n)], E_sc,
                   lambda e: e.activation(sq_k[:], eab[:], AF.Exp))
                op("scalar", [(E_sc, E_sc.n), (E_t, t_kv.get(t - 2))], E_sc,
                   lambda e, par=par: e.activation(khat[par][:], sq_k[:], AF.Ln, bias=cst[:, 1:2]))
                sc_khat[t] = E_sc.n

        def back_tile(t):
            t_q[t] = round_mm(t, 0, P_rA if t % 2 == 0 else P_rB,
                              [(E_v, v_evacX.get(t)), (E_v, v_kcopy.get(t - 1)),
                               (E_sc, sc_wq), (E_v, v_wq)])
            ladder(t, 0)
            t_k[t] = round_mm(t, D, P_rB if t % 2 == 0 else P_rA,
                              [(E_v, v_vev.get(t - 1))])
            ladder(t, 1)
            t_v[t] = round_mm(t, 2 * D, P_rA if t % 2 == 0 else P_rB,
                              [(E_sc, sc_qcopy[t])])
            op("vector", [(E_t, t_v[t]), (E_t, t_kv.get(t - 2)), (E_v, v_init)], E_v,
               lambda e, t=t: e.tensor_copy(
                   vtile[t % 2][:].rearrange("p (h c) -> p h c", h=H)[:, :, 0:64],
                   (P_rA if t % 2 == 0 else P_rB).ap().rearrange("p (h d) -> p h d", h=H)))
            v_vev[t] = E_v.n

        def trq_block(t):
            Pm = P_m1b if t % 2 == 0 else P_m2b
            depsq = [(E_v, v_qhat[t]), (E_sc, sc_evacQ.get(t - 2)),
                     (E_v, v_evacX.get(t + 2))]
            for dc in range(NDC):
                op("tensor", depsq if dc == 0 else [],
                   E_t if dc == NDC - 1 else None,
                   lambda e, dc=dc, t=t, Pm=Pm: e.transpose(
                       Pm[:, dc * 128:(dc + 1) * 128],
                       qhat[t % 2][:, dc * 128:(dc + 1) * 128], ident[:]))
            t_trq[t] = E_t.n
            op("scalar", [(E_t, E_t.n), (E_gd, gd_qt.get(t - 2))], E_sc,
               lambda e, t=t, Pm=Pm: e.copy(qh_st[t % 2][:], Pm[:, 0:1024]))
            sc_evacQ[t] = E_sc.n
            op("gpsimd", [(E_sc, E_sc.n)], E_gd,
               lambda e, t=t: e.dma_start(out=qT_scr[t], in_=qh_st[t % 2][:]), k=16)
            gd_qt[t] = E_gd.n

        def front_tile(t, i, fp):
            op("vector", [(E_sc, sc_sqrt[fp]), (E_v, v_stats[fp]),
                          (E_t, t_trx.get(t - 2))], E_v,
               lambda e, i=i, t=t: e.tensor_scalar(
                   xnb[t % 2][:], xb[t % 4], negmr[:, i:i + 1], rstd[:, i:i + 1],
                   ALU.add, ALU.mult))
            v_xn[t] = E_v.n
            Pm = P_m1b if t % 2 == 0 else P_m2b
            depsx = [(E_v, v_xn[t]), (E_sc, sc_evacQ.get(t - 2)),
                     (E_v, v_evacX.get(t - 2)), (E_s, s_ident)]
            for dc in range(NDC):
                op("tensor", depsx if dc == 0 else [],
                   E_t if dc == NDC - 1 else None,
                   lambda e, dc=dc, t=t, Pm=Pm: e.transpose(
                       Pm[:, dc * 128:(dc + 1) * 128],
                       xnb[t % 2][:, dc * 128:(dc + 1) * 128], ident[:]))
            t_trx[t] = E_t.n
            op("vector", [(E_t, E_t.n), (E_t, t_v.get(t - 4)), (E_gd, gd_xnt.get(t - 4))], E_v,
               lambda e, t=t, Pm=Pm: e.tensor_copy(xnT_t[t % 4][:], Pm[:, 0:1024]))
            v_evacX[t] = E_v.n
            op("gpsimd", [(E_v, v_evacX[t])], E_gd,
               lambda e, t=t: e.dma_start(
                   out=xnT_scr[:, t].transpose([1, 0, 2]),
                   in_=xnT_t[t % 4][:].rearrange("p (dc c) -> p dc c", dc=NDC)), k=16)
            gd_xnt[t] = E_gd.n

        for ip in range(NP + 2):
            fp = ip if ip < NP else None
            bp = ip - 1 if 1 <= ip <= NP else None
            kp = ip - 2 if ip >= 2 else None
            if fp is not None:
                front_x(2 * fp)
                front_x(2 * fp + 1)
            if kp is not None:
                trq_block(2 * kp)
                trq_block(2 * kp + 1)
                backT_kv(2 * kp)
                backT_kv(2 * kp + 1)
            if bp is not None:
                back_tile(2 * bp)
                back_tile(2 * bp + 1)
            if fp is not None:
                a, b = 2 * fp, 2 * fp + 1
                op("scalar", [(E_s, s_x[a]), (E_v, v_var.get(fp - 1))], E_sc,
                   lambda e, a=a: e.activation(sqdump, xb[a % 4], AF.Square, accum_out=ssq[:, 0:1]))
                op("scalar", [(E_s, s_x[b])], E_sc,
                   lambda e, b=b: e.activation(sqdump, xb[b % 4], AF.Square, accum_out=ssq[:, 1:2]))
                sc_sq[a] = sc_sq[b] = E_sc.n
                op("vector", [(E_s, s_x[b]), (E_v, v_var.get(fp - 1))], E_v,
                   lambda e, a=a: e.reduce_sum(
                       ssum[:], xb4.ap()[:, (a % 4) * D:(a % 4 + 2) * D]
                       .rearrange("p (i d) -> p i d", i=2), axis=AX.X))
                op("vector", [(E_v, E_v.n)], E_v,
                   lambda e: e.tensor_scalar_mul(mb[:], ssum[:], 1.0 / D))
                op("vector", [(E_v, E_v.n)], E_v,
                   lambda e: e.tensor_mul(mm2[:], mb[:], mb[:]))
                op("vector", [(E_v, E_v.n), (E_sc, sc_sq[b])], E_v,
                   lambda e: e.tensor_scalar(varb[:], ssq[:], 1.0 / D, EPS_LN, ALU.mult, ALU.add))
                op("vector", [(E_v, E_v.n)], E_v,
                   lambda e: e.tensor_sub(varb[:], varb[:], mm2[:]))
                v_var[fp] = E_v.n
                op("scalar", [(E_v, E_v.n)], E_sc,
                   lambda e: e.activation(stdv[:], varb[:], AF.Ln))
                op("scalar", [(E_sc, E_sc.n)], E_sc,
                   lambda e: e.activation(rstd[:], stdv[:], AF.Exp, scale=-0.5))
                sc_sqrt[fp] = E_sc.n
                op("vector", [(E_v, v_var[fp])], E_v,
                   lambda e: e.tensor_scalar_mul(negmr[:], mb[:], -1.0))
                v_stats[fp] = E_v.n
                front_tile(a, 0, fp)
                front_tile(b, 1, fp)
            if bp is not None:
                conv(bp)
        op("vector", [(E_t, t_kv[NT - 1])], E_v,
           lambda e: e.tensor_copy(
               cc_sb.rearrange("p (b q c) -> p b q c", b=2, q=4),
               P_kv.ap().rearrange("p (b f) -> p b f", b=2)[:, :, 0:260]
               .rearrange("p b (q c) -> p b q c", q=4)))
        v_ccevac = E_v.n
        op("gpsimd", [(E_v, E_v.n)], E_gd,
           lambda e: e.dma_start(out=cc_in.ap(), in_=cc_sb), k=16)
        op("gpsimd", [(E_gd, E_gd.n)], E_cc,
           lambda e: e.collective_compute(
               "AllReduce", ALU.add,
               replica_groups=[[0, 1], [2, 3], [4, 5], [6, 7]],
               ins=[cc_in.ap().opt()], outs=[cc_out.ap().opt()]))
        op("gpsimd", [(E_cc, 1)], E_gd,
           lambda e: e.dma_start(out=cc_sb, in_=cc_out.ap()), k=16)
        gd_ccout = E_gd.n

        ccv = cc_sb.rearrange("p (b q c) -> p b q c", b=2, q=4)

        def build_kvbd():
            deps = [(E_gd, gd_ccout), (E_v, v_init)]
            first = True
            for p in range(8):
                for eh in range(2):
                    op("vector", deps if first else [], E_v,
                       lambda e, p=p, eh=eh: e.tensor_scalar_mul(
                           kv_bd[eh * 64:(eh + 1) * 64, p * 130 + eh * 64: p * 130 + eh * 64 + 64],
                           ccv[eh * 64:(eh + 1) * 64, p // 4, p % 4, 0:64], SCALE_ATT))
                    first = False
                    op("vector", [], E_v,
                       lambda e, p=p, eh=eh: e.tensor_scalar_add(
                           kv_bd[eh * 64:(eh + 1) * 64, p * 130 + 128 + eh: p * 130 + 129 + eh],
                           ccv[eh * 64:(eh + 1) * 64, p // 4, p % 4, 64:65], EPS_ATT))
            return E_v.n

        v_kvbd = None
        wg_rd = [None, None]
        wu_rd = [None, None]
        xg_rd = [None, None]
        sg_rd = [None, None]
        silu_last = None
        hmul_last = None
        down_last = None
        for g in range(NTG):
            gpar = g % 2
            depsg = [(E_gd, gd_xnt[g * NTT + NTT - 1]), (E_t, xg_rd[gpar])]
            first = True
            for dc in range(NDC):
                for tb in range(NTT):
                    op("sync", depsg if first else [], E_s,
                       lambda e, dc=dc, tb=tb, g=g, gpar=gpar: e.dma_start(
                           out=xnT_g[gpar][:, dc * TG + tb * 128: dc * TG + (tb + 1) * 128],
                           in_=xnT_scr[dc, g * NTT + tb]), k=16)
                    first = False
            s_xg[g] = E_s.n
            for hc in range(NHC):
                hpar = hc % 2
                wgl_deps = [(E_t, wg_rd[hpar]), (E_gd, gd_swg.get(NHC - 1))]
                if g == 0 and hc < 2:
                    wgl_deps += [(E_v, v_xn.get(NT - 1)), (E_sc, sc_sq.get(NT - 1))]
                op("sync", wgl_deps, E_s,
                   lambda e, hc=hc, hpar=hpar: e.dma_start(out=wg_sb[hpar], in_=wg_bf[hc]), k=16)
                op("sync", [(E_t, wu_rd[hpar]), (E_gd, gd_swu.get(NHC - 1))], E_s,
                   lambda e, hc=hc, hpar=hpar: e.dma_start(out=wu_sb[hpar], in_=wu_bf[hc]), k=16)
                s_w[(g, hc)] = E_s.n
                depsgate = [(E_s, s_w[(g, hc)]), (E_s, s_xg[g]),
                            (E_sc, silu_last),
                            (E_sc, sc_evacQ.get(NT - 2)), (E_sc, sc_cwd.get(NHC - 1))]
                for dc in range(NDC):
                    op("tensor", depsgate if dc == 0 else [],
                       E_t if dc == NDC - 1 else None,
                       lambda e, dc=dc, hpar=hpar, gpar=gpar: e.matmul(
                           P_m1.ap()[:],
                           wg_sb[hpar][:, dc * 128:(dc + 1) * 128],
                           xnT_g[gpar][:, dc * TG:(dc + 1) * TG],
                           start=(dc == 0), stop=(dc == NDC - 1)))
                t_gate[(g, hc)] = E_t.n
                wg_rd[hpar] = E_t.n
                sdeps = [(E_t, E_t.n), (E_v, sg_rd[hpar])]
                if g == 0 and hc < 2:
                    sdeps.append((E_v, v_kmul.get(NT - 1)))
                op("scalar", sdeps, E_sc,
                   lambda e, hpar=hpar: e.activation(sg[hpar], P_m1.ap()[:], AF.Silu))
                sc_silu[(g, hc)] = E_sc.n
                silu_last = E_sc.n
                depsup = [(E_v, hmul_last)]
                if g == 0 and hc == 0:
                    depsup.append((E_sc, sc_evacQ.get(NT - 1)))
                for dc in range(NDC):
                    op("tensor", depsup if dc == 0 else [],
                       E_t if dc == NDC - 1 else None,
                       lambda e, dc=dc, hpar=hpar, gpar=gpar: e.matmul(
                           P_m2.ap()[:],
                           wu_sb[hpar][:, dc * 128:(dc + 1) * 128],
                           xnT_g[gpar][:, dc * TG:(dc + 1) * TG],
                           start=(dc == 0), stop=(dc == NDC - 1)))
                t_up[(g, hc)] = E_t.n
                wu_rd[hpar] = E_t.n
                xg_rd[gpar] = E_t.n
                depsh = [(E_t, E_t.n), (E_sc, sc_silu[(g, hc)]), (E_t, down_last)]
                if g == 0 and hc == 0:
                    depsh.append((E_t, t_v.get(NT - 1)))
                op("vector", depsh, E_v,
                   lambda e, hc=hc, hpar=hpar: e.tensor_tensor(
                       bigW[:, hc * TG:(hc + 1) * TG], P_m2.ap()[:], sg[hpar], ALU.mult))
                v_hmul[(g, hc)] = E_v.n
                sg_rd[hpar] = E_v.n
                hmul_last = E_v.n
            if g == 0:
                v_kvbd = build_kvbd()
            for tt in range(NTT):
                ttg = g * NTT + tt
                tpar = tt % 2
                P_dn = P_kv if tpar == 0 else P_rB
                P_de = P_rB if tpar == 0 else P_kv
                xdeps = [(E_v, v_ep2.get(ttg - 2))]
                if ttg < 2:
                    xdeps.append((E_v, v_xn.get(NT - 1)))
                    xdeps.append((E_sc, sc_sq.get(NT - 1)))
                op("sync", xdeps, E_s,
                   lambda e, ttg=ttg, tpar=tpar: e.dma_start(
                       out=xb[tpar], in_=x_in[ttg * 128:(ttg + 1) * 128, :]), k=16)
                s_x3[ttg] = E_s.n
                qtdeps = [(E_t, t_attn.get(ttg - 2)), (E_gd, gd_qt.get(ttg))]
                if ttg < 2:
                    qtdeps += [(E_sc, sc_cwd.get(NHC - 1)), (E_v, v_cwu.get(NHC - 1))]
                op("scalar", qtdeps, E_sd,
                   lambda e, ttg=ttg, tpar=tpar: e.dma_start(out=qT_t[tpar], in_=qT_scr[ttg]), k=16)
                sd_qt[ttg] = E_sd.n
                sfdeps = [(E_sd, E_sd.n)]
                if ttg == 0:
                    sfdeps.append((E_v, v_kmul.get(NT // 2 - 1)))
                op("scalar", sfdeps, E_sc,
                   lambda e, tpar=tpar: e.activation(eab[:, 0:D], qT_t[tpar], AF.Exp))
                op("scalar", [(E_sc, E_sc.n)], E_sc,
                   lambda e, tpar=tpar: e.activation(qT_t[tpar], eab[:, 0:D], AF.Ln, bias=cst[:, 1:2]))
                sc_qsoft[ttg] = E_sc.n
                depsd = [(E_v, v_hmul[(g, NHC - 1)]), (E_v, v_ep3.get(ttg - 2)),
                         (E_v, v_ep0.get(ttg - 1))]
                if ttg == 0:
                    depsd.append((E_v, v_ccevac))
                if ttg == 1:
                    depsd.append((E_v, v_vev.get(NT - 1)))
                first = True
                for hc in range(NHC):
                    for hf in range(2):
                        last = (hc == NHC - 1 and hf == 1)
                        op("tensor", depsd if first else [], E_t if last else None,
                           lambda e, hc=hc, hf=hf, tt=tt, P_dn=P_dn: e.matmul(
                               P_dn.ap()[:, hf * 512:(hf + 1) * 512],
                               bigW[:, hc * TG + tt * 128: hc * TG + (tt + 1) * 128],
                               wd_sb[:, hc * 1024 + hf * 512: hc * 1024 + (hf + 1) * 512],
                               start=(hc == 0), stop=(hc == NHC - 1)))
                        first = False
                t_down[ttg] = E_t.n
                down_last = E_t.n
                depsa = [(E_v, v_kvbd), (E_sc, sc_qsoft[ttg]),
                         (E_v, v_ep1.get(ttg - 1)), (E_v, v_ep3.get(ttg - 1))]
                if ttg == 0:
                    depsa += [(E_v, v_vev.get(NT - 1)), (E_v, v_kcopy.get(NT - 1))]
                first = True
                for p in range(8):
                    op("tensor", depsa if first else [], None,
                       lambda e, p=p, tpar=tpar: e.matmul(
                           P_rA.ap()[:, p * 128:(p + 1) * 128],
                           qT_t[tpar][:, p * 128:(p + 1) * 128],
                           kv_bd[:, p * 130: p * 130 + 128],
                           start=(p % 4 == 0), stop=(p % 4 == 3)))
                    first = False
                    op("tensor", [], E_t if p == 7 else None,
                       lambda e, p=p, tpar=tpar, P_de=P_de: e.matmul(
                           P_de.ap()[:, 2 * p: 2 * p + 2],
                           qT_t[tpar][:, p * 128:(p + 1) * 128],
                           kv_bd[:, p * 130 + 128: p * 130 + 130],
                           start=(p == 0), stop=(p == 7)))
                t_attn[ttg] = E_t.n
                epdeps = [(E_t, E_t.n), (E_v, v_ep1.get(ttg - 2))]
                if ttg == 0:
                    epdeps += [(E_v, v_qhat.get(NT - 1)), (E_v, v_kmul.get(NT - 1)),
                               (E_sc, sc_sq.get(NT - 1)), (E_sc, sc_wq), (E_v, v_wq)]
                op("vector", epdeps, E_v,
                   lambda e, tpar=tpar, P_de=P_de: e.reciprocal(rden[tpar][:], P_de.ap()[:, 0:16]))
                v_ep0[ttg] = E_v.n
                op("vector", [(E_v, E_v.n)], E_v,
                   lambda e, tpar=tpar: e.tensor_tensor(
                       ep_scr1.rearrange("p (a b c) -> p a b c", a=8, b=2),
                       P_rA.ap().rearrange("p (a b c) -> p a b c", a=8, b=2),
                       rden[tpar][:].rearrange("p (a b) -> p a b", a=8)
                       .unsqueeze(3).broadcast_to([128, 8, 2, 64]),
                       ALU.mult))
                v_ep1[ttg] = E_v.n
                op("vector", [(E_v, E_v.n), (E_s, s_x3[ttg])], E_v,
                   lambda e, tpar=tpar: e.tensor_add(ep_scr2, ep_scr1, xb[tpar]))
                v_ep2[ttg] = E_v.n
                ep3deps = [(E_v, E_v.n), (E_t, t_down[ttg]), (E_gd, gd_out.get(ttg - 2))]
                if ttg == 1:
                    ep3deps.append((E_v, v_kvbd))
                op("vector", ep3deps, E_v,
                   lambda e, tpar=tpar, P_dn=P_dn: e.tensor_add(outb[tpar][:], ep_scr2, P_dn.ap()[:]))
                v_ep3[ttg] = E_v.n
                op("gpsimd", [(E_v, E_v.n)], E_gd,
                   lambda e, ttg=ttg, tpar=tpar: e.dma_start(
                       out=y_out[ttg * 128:(ttg + 1) * 128, :], in_=outb[tpar][:]), k=16)
                gd_out[ttg] = E_gd.n

        fin = E_gd.n
        op("gpsimd", [(E_gd, fin)], None, lambda e: e.nop())

        block = es.enter_context(nc.Block())

        @block.sync
        def _(sync):
            for th in P.q["sync"]:
                th(sync)

        @block.vector
        def _(vector):
            for th in P.q["vector"]:
                th(vector)

        @block.scalar
        def _(scalar):
            for th in P.q["scalar"]:
                th(scalar)

        @block.tensor
        def _(tensor):
            for th in P.q["tensor"]:
                th(tensor)

        @block.gpsimd
        def _(gpsimd):
            for th in P.q["gpsimd"]:
                th(gpsimd)

    return nc


def _prep_inputs(L0_attn_x, L0_attn_W_qkv, L0_ffn_w_gate, L0_ffn_w_up, L0_ffn_w_down):
    x = np.asarray(L0_attn_x, dtype=np.float32)
    wq = np.ascontiguousarray(np.asarray(L0_attn_W_qkv, dtype=np.float32).T)
    wg = np.ascontiguousarray(
        np.asarray(L0_ffn_w_gate, dtype=np.int32).T.reshape(D, NHC, 128).transpose(1, 0, 2))
    wu = np.ascontiguousarray(
        np.asarray(L0_ffn_w_up, dtype=np.int32).T.reshape(D, NHC, 128).transpose(1, 0, 2))
    wd = np.ascontiguousarray(np.asarray(L0_ffn_w_down, dtype=np.int32).T)
    ident = np.eye(128, dtype=ml_dtypes.bfloat16)
    b, s, d = x.shape
    half = s // 2
    shards = []
    for c in range(8):
        sh = np.ascontiguousarray(x[c // 2, (c % 2) * half:(c % 2 + 1) * half, :])
        shards.append({"x": sh, "wqkv": wq, "wg": wg, "wu": wu, "wd": wd, "ident": ident})
    return shards, (b, s, d)


def kernel(L0_attn_x, L0_attn_W_qkv, L0_ffn_w_gate, L0_ffn_w_up, L0_ffn_w_down):
    in_maps, (b, s, d) = _prep_inputs(L0_attn_x, L0_attn_W_qkv, L0_ffn_w_gate,
                                      L0_ffn_w_up, L0_ffn_w_down)
    nc = build_program(TCORE=s // 2)
    res = run_bass_kernel_spmd(nc, in_maps, core_ids=list(range(8)))
    out = np.empty((b, s, d), dtype=np.float32)
    half = s // 2
    for c in range(8):
        out[c // 2, (c % 2) * half:(c % 2 + 1) * half, :] = res.results[c]["y"]
    return out


# version 13
# speedup vs baseline: 1.0084x; 1.0084x over previous
import contextlib
import numpy as np
import ml_dtypes
import concourse.bass as bass
import concourse.mybir as mybir
from concourse.bass_utils import run_bass_kernel_spmd

F32 = mybir.dt.float32
BF16 = mybir.dt.bfloat16
I32 = mybir.dt.int32
AF = mybir.ActivationFunctionType
ALU = mybir.AluOpType
AX = mybir.AxisListType

D = 1024
H = 16
HD = 64
FF = 3072
NHC = 24
NDC = 8
EPS_LN = 1e-5
EPS_ATT = 1e-6
EPS_L2 = 1e-12
SCALE_ATT = float(64 ** -0.5)
DEQ = 1.0 / 127.0

ENGS = ("sync", "vector", "scalar", "tensor", "gpsimd")


class Edge:
    def __init__(self, es, nc, name):
        self.h = es.enter_context(nc.semaphore(name))
        self.n = 0


class Plan:
    def __init__(self):
        self.q = {e: [] for e in ENGS}
        self.lastw = {e: {} for e in ENGS}

    def op(self, eng, deps, inc, build, k=1):
        dd = []
        for ed, v in deps:
            if ed is None or v is None or v <= 0:
                continue
            key = id(ed)
            if self.lastw[eng].get(key, 0) >= v:
                continue
            self.lastw[eng][key] = v
            dd.append((ed.h, v))
        icap = None
        if inc is not None:
            inc.n += k
            icap = (inc.h, k)

        def thunk(eo, dd=tuple(dd), icap=icap, build=build):
            for h, v in dd:
                eo.wait_ge(h, v)
            ins = build(eo)
            if icap is not None:
                ins.then_inc(icap[0], icap[1])

        self.q[eng].append(thunk)
        return inc.n if inc is not None else None


def build_program(TCORE=4096, detect_races=False):
    NT = TCORE // 128
    TG = 512
    NTG = TCORE // TG
    NTT = TG // 128

    nc = bass.Bass("TRN2", target_bir_lowering=False, debug=False,
                   detect_race_conditions=detect_races)

    x_in = nc.dram_tensor("x", [TCORE, D], F32, kind="ExternalInput").ap()
    wqkv = nc.dram_tensor("wqkv", [D, 3 * D], F32, kind="ExternalInput").ap()
    wg_i = nc.dram_tensor("wg", [NHC, D, 128], I32, kind="ExternalInput").ap()
    wu_i = nc.dram_tensor("wu", [NHC, D, 128], I32, kind="ExternalInput").ap()
    wd_i = nc.dram_tensor("wd", [FF, D], I32, kind="ExternalInput").ap()
    ident_in = nc.dram_tensor("ident", [128, 128], BF16, kind="ExternalInput").ap()
    y_out = nc.dram_tensor("y", [TCORE, D], F32, kind="ExternalOutput").ap()

    wg_bf = nc.dram_tensor("wg_bf", [NHC, 128, 1024], BF16).ap()
    wu_bf = nc.dram_tensor("wu_bf", [NHC, 128, 1024], BF16).ap()
    qT_scr = nc.dram_tensor("qT_scr", [NT, 128, 1024], BF16).ap()
    xnT_scr = nc.dram_tensor("xnT_scr", [NT, 128, 1024], BF16).ap()
    cc_in = nc.dram_tensor("cc_in", [128, 520], F32)
    cc_out = nc.dram_tensor("cc_out", [128, 520], F32)

    es = contextlib.ExitStack()
    with es:
        def sb(name, shape, dt):
            return es.enter_context(nc.sbuf_tensor(name, shape, dt))

        xb4 = sb("xb4", [128, 4 * D], F32)
        xb = [xb4.ap()[:, i * D:(i + 1) * D] for i in range(4)]
        qn = sb("qn", [128, D], F32)
        kn = sb("kn", [128, D], F32)
        eab = sb("eab", [128, D], BF16)
        ep2b = sb("ep2b", [128, D], F32)
        outb = [sb("outb0", [128, D], F32), sb("outb1", [128, D], F32)]
        sq_k = sb("sq_k", [128, D], BF16)
        xnb = [sb("xnb0", [128, D], BF16), sb("xnb1", [128, D], BF16)]
        qhat = [sb("qhat0", [128, D], BF16), sb("qhat1", [128, D], BF16)]
        khat = [sb("khat0", [128, D], BF16), sb("khat1", [128, D], BF16)]
        vtile = [sb("vt0", [128, H * 65], BF16), sb("vt1", [128, H * 65], BF16)]
        xnT_t = [sb("xnTt0", [128, D], BF16), sb("xnTt1", [128, D], BF16),
                 sb("xnTt2", [128, D], BF16), sb("xnTt3", [128, D], BF16)]
        qh_st = [sb("qhst0", [128, D], BF16), sb("qhst1", [128, D], BF16)]
        wcast = [sb("wc0", [128, 1024], BF16), sb("wc1", [128, 1024], BF16)]
        xnT_g = [sb("xnTg0", [128, NDC * TG], BF16), sb("xnTg1", [128, NDC * TG], BF16)]
        _xbt = xb4.ap()[:, 2 * D:4 * D].bitcast(BF16)
        wg_sb = [_xbt[:, 0:1024], _xbt[:, 1024:2048]]
        wu_sb = [_xbt[:, 2048:3072], _xbt[:, 3072:4096]]
        wd_sb = sb("wd_sb", [128, NHC * 1024], BF16)
        bigW = sb("bigW", [128, NDC * 3 * D], BF16)
        kv_bd = sb("kv_bd", [128, 8 * 130], BF16)
        ident = sb("ident_sb", [128, 128], BF16)
        wi = [sb("wi0", [128, 1024], I32), sb("wi1", [128, 1024], I32)]
        qT_t = [wi[0].ap().bitcast(BF16)[:, 0:1024], wi[1].ap().bitcast(BF16)[:, 0:1024]]
        cst = sb("cst", [128, 3], F32)
        ssum = sb("ssum", [128, 2], F32)
        ssq = sb("ssq", [128, 2], F32)
        mb = sb("mb", [128, 2], F32)
        mm2 = sb("mm2", [128, 2], F32)
        varb = sb("varb", [128, 2], F32)
        stdv = sb("stdv", [128, 2], F32)
        rstd = sb("rstd", [128, 2], F32)
        negmr = sb("negmr", [128, 2], F32)
        ssq_q = sb("ssq_q", [128, H], F32)
        ssq_k = sb("ssq_k", [128, H], F32)
        nrm2_q = sb("nrm2_q", [128, H], F32)
        nrm2_k = sb("nrm2_k", [128, H], F32)
        rst_q = sb("rst_q", [128, H], F32)
        rst_k = sb("rst_k", [128, H], F32)
        rden = [sb("rden0", [128, H], F32), sb("rden1", [128, H], F32)]

        sqdump = outb[0].ap().bitcast(BF16)[:, 0:D]
        cc_sb = outb[1].ap()[:, 0:520]
        sg = [kn.ap()[:, 0:512], kn.ap()[:, 512:1024]]
        ep_scr1 = qn.ap()[:, 0:D]
        ep_scr2 = ep2b.ap()[:]
        P_rA = es.enter_context(nc.psum_tensor("P_rA", [128, 1024], F32))
        P_rB = es.enter_context(nc.psum_tensor("P_rB", [128, 1024], F32))
        P_kv = es.enter_context(nc.psum_tensor("P_kv", [128, 1024], F32))
        P_m1 = es.enter_context(nc.psum_tensor("P_m1", [128, 512], F32))
        P_m2 = es.enter_context(nc.psum_tensor("P_m2", [128, 512], F32))
        P_m1b = P_m1.ap().bitcast(BF16)
        P_m2b = P_m2.ap().bitcast(BF16)

        E_s = Edge(es, nc, "E_s")
        E_sd = Edge(es, nc, "E_sd")
        E_gd = Edge(es, nc, "E_gd")
        E_t = Edge(es, nc, "E_t")
        E_v = Edge(es, nc, "E_v")
        E_sc = Edge(es, nc, "E_sc")
        E_cc = Edge(es, nc, "E_cc")
        E_g = Edge(es, nc, "E_g")

        P = Plan()
        op = P.op

        s_x = {}; sc_sq = {}; v_var = {}; sc_sqrt = {}; v_stats = {}; sc_xn = {}
        t_trx = {}; v_evacX = {}; gd_xnt = {}
        t_q = {}; sc_qsq = {}; v_gq = {}; sc_nq = {}; v_qmul = {}; sc_expq = {}; sc_qhat = {}
        t_k = {}; sc_ksq = {}; v_gk = {}; sc_nk = {}; v_kmul = {}; sc_expk = {}; sc_khat = {}
        t_v = {}; v_vev = {}; t_kv = {}; t_trq = {}; v_evacQ = {}; gd_qt = {}
        gd_wg = {}; sc_cwg = {}; gd_swg = {}
        gd_wu = {}; v_cwu = {}; gd_swu = {}
        gd_wd = {}; sc_cwd = {}
        t_gate = {}; sc_silu = {}; t_up = {}; v_hmul = {}
        s_w = {}; s_xg = {}; s_x3 = {}; sd_qt = {}
        t_down = {}; t_attn = {}
        v_ep0 = {}; v_ep1 = {}; v_ep2 = {}; v_ep3 = {}; gd_out = {}

        op("sync", [], E_s, lambda e: e.dma_start(out=ident[:], in_=ident_in), k=16)
        s_ident = E_s.n
        op("vector", [], E_v, lambda e: e.memset(vtile[0][:], 1.0))
        op("vector", [], E_v, lambda e: e.memset(vtile[1][:], 1.0))
        op("vector", [], E_v, lambda e: e.memset(kv_bd[:], 0.0))
        op("vector", [], E_v, lambda e: e.memset(cst[:, 0:1], EPS_LN))
        op("vector", [], E_v, lambda e: e.memset(cst[:, 1:2], 1.0))
        op("vector", [], E_v, lambda e: e.memset(cst[:, 2:3], EPS_L2 * EPS_L2))
        v_init = E_v.n

        wq_cast = {}
        for kk in range(NDC * 3):
            dc, s = divmod(kk, 3)
            par = kk % 2
            deps = [(E_sd, None)]
            if kk >= 2:
                pe, pv = wq_cast[kk - 2]
                deps = [((E_sc if pe == "scalar" else E_v), pv)]
            op("scalar", deps, E_sd,
               lambda e, dc=dc, s=s, par=par: e.dma_start(
                   out=outb[par][:], in_=wqkv[dc * 128:(dc + 1) * 128, s * 1024:(s + 1) * 1024]), k=16)
            ld = E_sd.n
            eng = "scalar" if (kk % 2 == 0) else "vector"
            ed = E_sc if eng == "scalar" else E_v
            col = dc * 3 * D + s * 1024
            if eng == "scalar":
                op(eng, [(E_sd, ld)], ed,
                   lambda e, par=par, col=col: e.activation(
                       bigW[:, col:col + 1024], outb[par][:], AF.Copy))
            else:
                op(eng, [(E_sd, ld)], ed,
                   lambda e, par=par, col=col: e.tensor_copy(bigW[:, col:col + 1024], outb[par][:]))
            wq_cast[kk] = (eng, ed.n)
        sc_wq = E_sc.n
        v_wq = E_v.n

        def conv_sched(t):
            return range(t * NHC // NT, (t + 1) * NHC // NT)

        sc_qcopy = {}
        v_kcopy = {}
        v_xn = {}
        sc_evacQ = {}
        sc_qsoft = {}
        v_qhat = {}
        sc_khat = {}
        v_gq = {}
        v_gk = {}
        NP = NT // 2

        def front_x(t):
            op("sync", [(E_v, v_xn.get(t - 4)), (E_sc, sc_sq.get(t - 4))], E_s,
               lambda e, t=t: e.dma_start(out=xb[t % 4], in_=x_in[t * 128:(t + 1) * 128, :]), k=16)
            s_x[t] = E_s.n

        def round_mm(t, base, Pr, deps):
            tpar = t % 4
            for dc in range(NDC):
                for hf in range(2):
                    ed = E_t if (dc == NDC - 1 and hf == 1) else None
                    op("tensor", deps if (dc == 0 and hf == 0) else [], ed,
                       lambda e, dc=dc, hf=hf, tpar=tpar, Pr=Pr, base=base: e.matmul(
                           Pr.ap()[:, hf * 512:(hf + 1) * 512],
                           xnT_t[tpar][:, dc * 128:(dc + 1) * 128],
                           bigW[:, dc * 3 * D + base + hf * 512: dc * 3 * D + base + (hf + 1) * 512],
                           start=(dc == 0), stop=(dc == NDC - 1)))
            return E_t.n

        def kv_col(p):
            return (p // 4) * 512 + (p % 4) * 65

        def backT_kv(t):
            deps = [(E_sc, sc_khat[t]), (E_v, v_vev[t])]
            first = True
            for p in range(8):
                for eh in range(2):
                    h = 2 * p + eh
                    st = (t == 0) and (p % 4 == 0)
                    sp = (t == NT - 1) and (p % 4 == 3)
                    last = (p == 7) and (eh == 1)
                    op("tensor", deps if first else [], E_t if last else None,
                       lambda e, p=p, eh=eh, h=h, st=st, sp=sp, t=t: e.matmul(
                           P_kv.ap()[eh * 64:(eh + 1) * 64, kv_col(p):kv_col(p) + 65],
                           khat[t % 2][:, h * 64:(h + 1) * 64],
                           vtile[t % 2][:, h * 65:(h + 1) * 65],
                           start=st, stop=sp, skip_group_check=True))
                    first = False
            t_kv[t] = E_t.n

        def conv(tp):
            for hc in range(tp * NHC // NP, (tp + 1) * NHC // NP):
                op("gpsimd", [(E_sc, sc_cwd.get(hc - 1))], E_gd,
                   lambda e, hc=hc: e.dma_start(
                       out=wi[0][:].rearrange("p (dc c) -> p dc c", dc=NDC),
                       in_=wg_i[hc].rearrange("(dc p) c -> p dc c", p=128)), k=16)
                gd_wg[hc] = E_gd.n
                op("scalar", [(E_gd, E_gd.n), (E_gd, gd_swg.get(hc - 1))], E_sc,
                   lambda e: e.activation(wcast[0][:], wi[0][:], AF.Copy, scale=DEQ))
                sc_cwg[hc] = E_sc.n
                op("gpsimd", [(E_sc, E_sc.n)], E_gd,
                   lambda e, hc=hc: e.dma_start(out=wg_bf[hc], in_=wcast[0][:]), k=16)
                gd_swg[hc] = E_gd.n
                op("gpsimd", [(E_v, v_cwu.get(hc - 1))], E_gd,
                   lambda e, hc=hc: e.dma_start(
                       out=wi[1][:].rearrange("p (dc c) -> p dc c", dc=NDC),
                       in_=wu_i[hc].rearrange("(dc p) c -> p dc c", p=128)), k=16)
                gd_wu[hc] = E_gd.n
                op("vector", [(E_gd, E_gd.n), (E_gd, gd_swu.get(hc - 1))], E_v,
                   lambda e: e.tensor_scalar_mul(wcast[1][:], wi[1][:], DEQ))
                v_cwu[hc] = E_v.n
                op("gpsimd", [(E_v, E_v.n)], E_gd,
                   lambda e, hc=hc: e.dma_start(out=wu_bf[hc], in_=wcast[1][:]), k=16)
                gd_swu[hc] = E_gd.n
                op("gpsimd", [(E_sc, sc_cwg[hc])], E_gd,
                   lambda e, hc=hc: e.dma_start(out=wi[0][:], in_=wd_i[hc * 128:(hc + 1) * 128, :]), k=16)
                op("scalar", [(E_gd, E_gd.n)], E_sc,
                   lambda e, hc=hc: e.activation(wd_sb[:, hc * 1024:(hc + 1) * 1024], wi[0][:],
                                                 AF.Copy, scale=DEQ))
                sc_cwd[hc] = E_sc.n
                g_conv[hc] = sc_cwd[hc]

        g_conv = {}

        def ladder(t, which):
            par = t % 2
            Pr = ((P_rA if t % 2 == 0 else P_rB) if which == 0
                  else (P_rB if t % 2 == 0 else P_rA))
            if which == 0:
                op("scalar", [(E_t, t_q[t]), (E_v, v_qhat.get(t - 1))], E_sc,
                   lambda e, Pr=Pr: e.copy(qn[:], Pr.ap()[:]))
                sc_qcopy[t] = E_sc.n
                op("scalar", [(E_sc, E_sc.n), (E_v, v_gq.get(t - 1))], E_sc,
                   lambda e: e.activation(eab[:], qn[:], AF.Square))
                op("vector", [(E_sc, E_sc.n)], E_v,
                   lambda e: e.reduce_sum(
                       ssq_q[:], eab[:].rearrange("p (h d) -> p h d", h=H), axis=AX.X))
                v_gq[t] = E_v.n
                op("scalar", [(E_v, E_v.n)], E_sc,
                   lambda e: e.activation(nrm2_q[:], ssq_q[:], AF.Ln, bias=cst[:, 2:3]))
                op("scalar", [(E_sc, E_sc.n)], E_sc,
                   lambda e: e.activation(rst_q[:], nrm2_q[:], AF.Exp, scale=-0.5))
                op("vector", [(E_sc, E_sc.n), (E_t, t_trq.get(t - 2))], E_v,
                   lambda e, par=par: e.tensor_tensor(
                       qhat[par][:].rearrange("p (h d) -> p h d", h=H),
                       qn[:].rearrange("p (h d) -> p h d", h=H),
                       rst_q[:].unsqueeze(2).broadcast_to([128, H, HD]),
                       ALU.mult))
                v_qhat[t] = E_v.n
            else:
                op("vector", [(E_t, t_k[t]), (E_v, v_kmul.get(t - 1))], E_v,
                   lambda e, Pr=Pr: e.tensor_copy(kn[:], Pr.ap()[:]))
                v_kcopy[t] = E_v.n
                op("scalar", [(E_v, E_v.n), (E_v, v_gk.get(t - 1))], E_sc,
                   lambda e: e.activation(sq_k[:], kn[:], AF.Square))
                op("vector", [(E_sc, E_sc.n)], E_v,
                   lambda e: e.reduce_sum(
                       ssq_k[:], sq_k[:].rearrange("p (h d) -> p h d", h=H), axis=AX.X))
                v_gk[t] = E_v.n
                op("scalar", [(E_v, E_v.n)], E_sc,
                   lambda e: e.activation(nrm2_k[:], ssq_k[:], AF.Ln, bias=cst[:, 2:3]))
                op("scalar", [(E_sc, E_sc.n)], E_sc,
                   lambda e: e.activation(rst_k[:], nrm2_k[:], AF.Exp, scale=-0.5))
                op("vector", [(E_sc, E_sc.n)], E_v,
                   lambda e: e.tensor_tensor(
                       eab[:].rearrange("p (h d) -> p h d", h=H),
                       kn[:].rearrange("p (h d) -> p h d", h=H),
                       rst_k[:].unsqueeze(2).broadcast_to([128, H, HD]),
                       ALU.mult))
                v_kmul[t] = E_v.n
                op("scalar", [(E_v, E_v.n)], E_sc,
                   lambda e: e.activation(sq_k[:], eab[:], AF.Exp))
                op("scalar", [(E_sc, E_sc.n), (E_t, t_kv.get(t - 2))], E_sc,
                   lambda e, par=par: e.activation(khat[par][:], sq_k[:], AF.Ln, bias=cst[:, 1:2]))
                sc_khat[t] = E_sc.n

        def back_tile(t):
            t_q[t] = round_mm(t, 0, P_rA if t % 2 == 0 else P_rB,
                              [(E_s, s_xnl[t]), (E_v, v_kcopy.get(t - 1)),
                               (E_sc, sc_wq), (E_v, v_wq)])
            ladder(t, 0)
            t_k[t] = round_mm(t, D, P_rB if t % 2 == 0 else P_rA,
                              [(E_v, v_vev.get(t - 1))])
            ladder(t, 1)
            t_v[t] = round_mm(t, 2 * D, P_rA if t % 2 == 0 else P_rB,
                              [(E_sc, sc_qcopy[t])])
            op("vector", [(E_t, t_v[t]), (E_t, t_kv.get(t - 2)), (E_v, v_init)], E_v,
               lambda e, t=t: e.tensor_copy(
                   vtile[t % 2][:].rearrange("p (h c) -> p h c", h=H)[:, :, 0:64],
                   (P_rA if t % 2 == 0 else P_rB).ap().rearrange("p (h d) -> p h d", h=H)))
            v_vev[t] = E_v.n

        def trq_block(t):
            Pm = P_m1b if t % 2 == 0 else P_m2b
            depsq = [(E_v, v_qhat[t]), (E_sc, sc_evacQ.get(t - 2))]
            if t < 2:
                depsq.append((E_v, v_evacX.get(NT - 1)))
            for dc in range(NDC):
                op("tensor", depsq if dc == 0 else [],
                   E_t if dc == NDC - 1 else None,
                   lambda e, dc=dc, t=t, Pm=Pm: e.transpose(
                       Pm[:, dc * 128:(dc + 1) * 128],
                       qhat[t % 2][:, dc * 128:(dc + 1) * 128], ident[:]))
            t_trq[t] = E_t.n
            op("scalar", [(E_t, E_t.n), (E_gd, gd_qt.get(t - 2))], E_sc,
               lambda e, t=t, Pm=Pm: e.copy(qh_st[t % 2][:], Pm[:, 0:1024]))
            sc_evacQ[t] = E_sc.n
            op("gpsimd", [(E_sc, E_sc.n)], E_gd,
               lambda e, t=t: e.dma_start(out=qT_scr[t], in_=qh_st[t % 2][:]), k=16)
            gd_qt[t] = E_gd.n

        def front_tile(t, i, fp):
            op("vector", [(E_sc, sc_sqrt[fp]), (E_v, v_stats[fp]),
                          (E_t, t_trx.get(t - 2))], E_v,
               lambda e, i=i, t=t: e.tensor_scalar(
                   xnb[t % 2][:], xb[t % 4], negmr[:, i:i + 1], rstd[:, i:i + 1],
                   ALU.add, ALU.mult))
            v_xn[t] = E_v.n
            Pm = P_m1b if t % 2 == 0 else P_m2b
            depsx = [(E_v, v_xn[t]), (E_v, v_evacX.get(t - 2)), (E_s, s_ident)]
            for dc in range(NDC):
                op("tensor", depsx if dc == 0 else [],
                   E_t if dc == NDC - 1 else None,
                   lambda e, dc=dc, t=t, Pm=Pm: e.transpose(
                       Pm[:, dc * 128:(dc + 1) * 128],
                       xnb[t % 2][:, dc * 128:(dc + 1) * 128], ident[:]))
            t_trx[t] = E_t.n
            op("vector", [(E_t, E_t.n), (E_gd, gd_xnt.get(t - 4))], E_v,
               lambda e, t=t, Pm=Pm: e.tensor_copy(xnT_t[t % 4][:], Pm[:, 0:1024]))
            v_evacX[t] = E_v.n
            op("gpsimd", [(E_v, v_evacX[t])], E_gd,
               lambda e, t=t: e.dma_start(out=xnT_scr[t], in_=xnT_t[t % 4][:]), k=16)
            gd_xnt[t] = E_gd.n

        for fp in range(NP):
            front_x(2 * fp)
            front_x(2 * fp + 1)
            if True:
                a, b = 2 * fp, 2 * fp + 1
                op("scalar", [(E_s, s_x[a]), (E_v, v_var.get(fp - 1))], E_sc,
                   lambda e, a=a: e.activation(sqdump, xb[a % 4], AF.Square, accum_out=ssq[:, 0:1]))
                op("scalar", [(E_s, s_x[b])], E_sc,
                   lambda e, b=b: e.activation(sqdump, xb[b % 4], AF.Square, accum_out=ssq[:, 1:2]))
                sc_sq[a] = sc_sq[b] = E_sc.n
                op("vector", [(E_s, s_x[b]), (E_v, v_var.get(fp - 1))], E_v,
                   lambda e, a=a: e.reduce_sum(
                       ssum[:], xb4.ap()[:, (a % 4) * D:(a % 4 + 2) * D]
                       .rearrange("p (i d) -> p i d", i=2), axis=AX.X))
                op("vector", [(E_v, E_v.n)], E_v,
                   lambda e: e.tensor_scalar_mul(mb[:], ssum[:], 1.0 / D))
                op("vector", [(E_v, E_v.n)], E_v,
                   lambda e: e.tensor_mul(mm2[:], mb[:], mb[:]))
                op("vector", [(E_v, E_v.n), (E_sc, sc_sq[b])], E_v,
                   lambda e: e.tensor_scalar(varb[:], ssq[:], 1.0 / D, EPS_LN, ALU.mult, ALU.add))
                op("vector", [(E_v, E_v.n)], E_v,
                   lambda e: e.tensor_sub(varb[:], varb[:], mm2[:]))
                v_var[fp] = E_v.n
                op("scalar", [(E_v, E_v.n)], E_sc,
                   lambda e: e.activation(stdv[:], varb[:], AF.Ln))
                op("scalar", [(E_sc, E_sc.n)], E_sc,
                   lambda e: e.activation(rstd[:], stdv[:], AF.Exp, scale=-0.5))
                sc_sqrt[fp] = E_sc.n
                op("vector", [(E_v, v_var[fp])], E_v,
                   lambda e: e.tensor_scalar_mul(negmr[:], mb[:], -1.0))
                v_stats[fp] = E_v.n
                front_tile(a, 0, fp)
                front_tile(b, 1, fp)
            conv(fp)

        s_xnl = {}

        def xnl(t):
            op("sync", [(E_t, t_v.get(t - 4)), (E_gd, gd_xnt[t])], E_s,
               lambda e, t=t: e.dma_start(out=xnT_t[t % 4][:], in_=xnT_scr[t]), k=16)
            s_xnl[t] = E_s.n

        for ip in range(NP + 2):
            lp = ip if ip < NP else None
            bp = ip - 1 if 1 <= ip <= NP else None
            kp = ip - 2 if ip >= 2 else None
            if lp is not None:
                xnl(2 * lp)
                xnl(2 * lp + 1)
            if kp is not None:
                trq_block(2 * kp)
                trq_block(2 * kp + 1)
                backT_kv(2 * kp)
                backT_kv(2 * kp + 1)
            if bp is not None:
                back_tile(2 * bp)
                back_tile(2 * bp + 1)
        op("vector", [(E_t, t_kv[NT - 1])], E_v,
           lambda e: e.tensor_copy(
               cc_sb.rearrange("p (b q c) -> p b q c", b=2, q=4),
               P_kv.ap().rearrange("p (b f) -> p b f", b=2)[:, :, 0:260]
               .rearrange("p b (q c) -> p b q c", q=4)))
        v_ccevac = E_v.n
        op("gpsimd", [(E_v, E_v.n)], E_gd,
           lambda e: e.dma_start(out=cc_in.ap(), in_=cc_sb), k=16)
        op("gpsimd", [(E_gd, E_gd.n)], E_cc,
           lambda e: e.collective_compute(
               "AllReduce", ALU.add,
               replica_groups=[[0, 1], [2, 3], [4, 5], [6, 7]],
               ins=[cc_in.ap().opt()], outs=[cc_out.ap().opt()]))
        op("gpsimd", [(E_cc, 1)], E_gd,
           lambda e: e.dma_start(out=cc_sb, in_=cc_out.ap()), k=16)
        gd_ccout = E_gd.n

        ccv = cc_sb.rearrange("p (b q c) -> p b q c", b=2, q=4)

        def build_kvbd():
            deps = [(E_gd, gd_ccout), (E_v, v_init)]
            first = True
            for p in range(8):
                for eh in range(2):
                    op("vector", deps if first else [], E_v,
                       lambda e, p=p, eh=eh: e.tensor_scalar_mul(
                           kv_bd[eh * 64:(eh + 1) * 64, p * 130 + eh * 64: p * 130 + eh * 64 + 64],
                           ccv[eh * 64:(eh + 1) * 64, p // 4, p % 4, 0:64], SCALE_ATT))
                    first = False
                    op("vector", [], E_v,
                       lambda e, p=p, eh=eh: e.tensor_scalar_add(
                           kv_bd[eh * 64:(eh + 1) * 64, p * 130 + 128 + eh: p * 130 + 129 + eh],
                           ccv[eh * 64:(eh + 1) * 64, p // 4, p % 4, 64:65], EPS_ATT))
            return E_v.n

        v_kvbd = None
        wg_rd = [None, None]
        wu_rd = [None, None]
        xg_rd = [None, None]
        sg_rd = [None, None]
        silu_last = None
        hmul_last = None
        down_last = None
        for g in range(NTG):
            gpar = g % 2
            depsg = [(E_gd, gd_xnt[g * NTT + NTT - 1]), (E_t, xg_rd[gpar])]
            first = True
            for tb in range(NTT):
                op("sync", depsg if first else [], E_s,
                   lambda e, tb=tb, g=g, gpar=gpar: e.dma_start(
                       out=xnT_g[gpar][:].rearrange("p (dc tb c) -> p dc tb c", dc=NDC, tb=NTT)
                       [:, :, tb, :],
                       in_=xnT_scr[g * NTT + tb].rearrange("p (dc c) -> p dc c", dc=NDC)), k=16)
                first = False
            s_xg[g] = E_s.n
            for hc in range(NHC):
                hpar = hc % 2
                wgl_deps = [(E_t, wg_rd[hpar]), (E_gd, gd_swg.get(NHC - 1))]
                if g == 0 and hc < 2:
                    wgl_deps += [(E_v, v_xn.get(NT - 1)), (E_sc, sc_sq.get(NT - 1))]
                op("sync", wgl_deps, E_s,
                   lambda e, hc=hc, hpar=hpar: e.dma_start(out=wg_sb[hpar], in_=wg_bf[hc]), k=16)
                op("sync", [(E_t, wu_rd[hpar]), (E_gd, gd_swu.get(NHC - 1))], E_s,
                   lambda e, hc=hc, hpar=hpar: e.dma_start(out=wu_sb[hpar], in_=wu_bf[hc]), k=16)
                s_w[(g, hc)] = E_s.n
                depsgate = [(E_s, s_w[(g, hc)]), (E_s, s_xg[g]),
                            (E_sc, silu_last),
                            (E_sc, sc_evacQ.get(NT - 2)), (E_sc, sc_cwd.get(NHC - 1))]
                for dc in range(NDC):
                    op("tensor", depsgate if dc == 0 else [],
                       E_t if dc == NDC - 1 else None,
                       lambda e, dc=dc, hpar=hpar, gpar=gpar: e.matmul(
                           P_m1.ap()[:],
                           wg_sb[hpar][:, dc * 128:(dc + 1) * 128],
                           xnT_g[gpar][:, dc * TG:(dc + 1) * TG],
                           start=(dc == 0), stop=(dc == NDC - 1)))
                t_gate[(g, hc)] = E_t.n
                wg_rd[hpar] = E_t.n
                sdeps = [(E_t, E_t.n), (E_v, sg_rd[hpar])]
                if g == 0 and hc < 2:
                    sdeps.append((E_v, v_kmul.get(NT - 1)))
                op("scalar", sdeps, E_sc,
                   lambda e, hpar=hpar: e.activation(sg[hpar], P_m1.ap()[:], AF.Silu))
                sc_silu[(g, hc)] = E_sc.n
                silu_last = E_sc.n
                depsup = [(E_v, hmul_last)]
                if g == 0 and hc == 0:
                    depsup.append((E_sc, sc_evacQ.get(NT - 1)))
                for dc in range(NDC):
                    op("tensor", depsup if dc == 0 else [],
                       E_t if dc == NDC - 1 else None,
                       lambda e, dc=dc, hpar=hpar, gpar=gpar: e.matmul(
                           P_m2.ap()[:],
                           wu_sb[hpar][:, dc * 128:(dc + 1) * 128],
                           xnT_g[gpar][:, dc * TG:(dc + 1) * TG],
                           start=(dc == 0), stop=(dc == NDC - 1)))
                t_up[(g, hc)] = E_t.n
                wu_rd[hpar] = E_t.n
                xg_rd[gpar] = E_t.n
                depsh = [(E_t, E_t.n), (E_sc, sc_silu[(g, hc)]), (E_t, down_last)]
                if g == 0 and hc == 0:
                    depsh.append((E_t, t_v.get(NT - 1)))
                op("vector", depsh, E_v,
                   lambda e, hc=hc, hpar=hpar: e.tensor_tensor(
                       bigW[:, hc * TG:(hc + 1) * TG], P_m2.ap()[:], sg[hpar], ALU.mult))
                v_hmul[(g, hc)] = E_v.n
                sg_rd[hpar] = E_v.n
                hmul_last = E_v.n
            if g == 0:
                v_kvbd = build_kvbd()
            for tt in range(NTT):
                ttg = g * NTT + tt
                tpar = tt % 2
                P_dn = P_kv if tpar == 0 else P_rB
                P_de = P_rB if tpar == 0 else P_kv
                xdeps = [(E_v, v_ep2.get(ttg - 2))]
                if ttg < 2:
                    xdeps.append((E_v, v_xn.get(NT - 1)))
                    xdeps.append((E_sc, sc_sq.get(NT - 1)))
                op("sync", xdeps, E_s,
                   lambda e, ttg=ttg, tpar=tpar: e.dma_start(
                       out=xb[tpar], in_=x_in[ttg * 128:(ttg + 1) * 128, :]), k=16)
                s_x3[ttg] = E_s.n
                qtdeps = [(E_t, t_attn.get(ttg - 2)), (E_gd, gd_qt.get(ttg))]
                if ttg < 2:
                    qtdeps += [(E_sc, sc_cwd.get(NHC - 1)), (E_v, v_cwu.get(NHC - 1))]
                op("scalar", qtdeps, E_sd,
                   lambda e, ttg=ttg, tpar=tpar: e.dma_start(out=qT_t[tpar], in_=qT_scr[ttg]), k=16)
                sd_qt[ttg] = E_sd.n
                sfdeps = [(E_sd, E_sd.n)]
                if ttg == 0:
                    sfdeps.append((E_v, v_kmul.get(NT // 2 - 1)))
                op("scalar", sfdeps, E_sc,
                   lambda e, tpar=tpar: e.activation(eab[:, 0:D], qT_t[tpar], AF.Exp))
                op("scalar", [(E_sc, E_sc.n)], E_sc,
                   lambda e, tpar=tpar: e.activation(qT_t[tpar], eab[:, 0:D], AF.Ln, bias=cst[:, 1:2]))
                sc_qsoft[ttg] = E_sc.n
                depsd = [(E_v, v_hmul[(g, NHC - 1)]), (E_v, v_ep3.get(ttg - 2)),
                         (E_v, v_ep0.get(ttg - 1))]
                if ttg == 0:
                    depsd.append((E_v, v_ccevac))
                if ttg == 1:
                    depsd.append((E_v, v_vev.get(NT - 1)))
                first = True
                for hc in range(NHC):
                    for hf in range(2):
                        last = (hc == NHC - 1 and hf == 1)
                        op("tensor", depsd if first else [], E_t if last else None,
                           lambda e, hc=hc, hf=hf, tt=tt, P_dn=P_dn: e.matmul(
                               P_dn.ap()[:, hf * 512:(hf + 1) * 512],
                               bigW[:, hc * TG + tt * 128: hc * TG + (tt + 1) * 128],
                               wd_sb[:, hc * 1024 + hf * 512: hc * 1024 + (hf + 1) * 512],
                               start=(hc == 0), stop=(hc == NHC - 1)))
                        first = False
                t_down[ttg] = E_t.n
                down_last = E_t.n
                depsa = [(E_v, v_kvbd), (E_sc, sc_qsoft[ttg]),
                         (E_v, v_ep1.get(ttg - 1)), (E_v, v_ep3.get(ttg - 1))]
                if ttg == 0:
                    depsa += [(E_v, v_vev.get(NT - 1)), (E_v, v_kcopy.get(NT - 1))]
                first = True
                for p in range(8):
                    op("tensor", depsa if first else [], None,
                       lambda e, p=p, tpar=tpar: e.matmul(
                           P_rA.ap()[:, p * 128:(p + 1) * 128],
                           qT_t[tpar][:, p * 128:(p + 1) * 128],
                           kv_bd[:, p * 130: p * 130 + 128],
                           start=(p % 4 == 0), stop=(p % 4 == 3)))
                    first = False
                    op("tensor", [], E_t if p == 7 else None,
                       lambda e, p=p, tpar=tpar, P_de=P_de: e.matmul(
                           P_de.ap()[:, 2 * p: 2 * p + 2],
                           qT_t[tpar][:, p * 128:(p + 1) * 128],
                           kv_bd[:, p * 130 + 128: p * 130 + 130],
                           start=(p == 0), stop=(p == 7)))
                t_attn[ttg] = E_t.n
                epdeps = [(E_t, E_t.n), (E_v, v_ep1.get(ttg - 2))]
                if ttg == 0:
                    epdeps += [(E_v, v_qhat.get(NT - 1)), (E_v, v_kmul.get(NT - 1)),
                               (E_sc, sc_sq.get(NT - 1)), (E_sc, sc_wq), (E_v, v_wq)]
                op("vector", epdeps, E_v,
                   lambda e, tpar=tpar, P_de=P_de: e.reciprocal(rden[tpar][:], P_de.ap()[:, 0:16]))
                v_ep0[ttg] = E_v.n
                op("vector", [(E_v, E_v.n)], E_v,
                   lambda e, tpar=tpar: e.tensor_tensor(
                       ep_scr1.rearrange("p (a b c) -> p a b c", a=8, b=2),
                       P_rA.ap().rearrange("p (a b c) -> p a b c", a=8, b=2),
                       rden[tpar][:].rearrange("p (a b) -> p a b", a=8)
                       .unsqueeze(3).broadcast_to([128, 8, 2, 64]),
                       ALU.mult))
                v_ep1[ttg] = E_v.n
                op("vector", [(E_v, E_v.n), (E_s, s_x3[ttg])], E_v,
                   lambda e, tpar=tpar: e.tensor_add(ep_scr2, ep_scr1, xb[tpar]))
                v_ep2[ttg] = E_v.n
                ep3deps = [(E_v, E_v.n), (E_t, t_down[ttg]), (E_gd, gd_out.get(ttg - 2))]
                if ttg == 1:
                    ep3deps.append((E_v, v_kvbd))
                op("vector", ep3deps, E_v,
                   lambda e, tpar=tpar, P_dn=P_dn: e.tensor_add(outb[tpar][:], ep_scr2, P_dn.ap()[:]))
                v_ep3[ttg] = E_v.n
                op("gpsimd", [(E_v, E_v.n)], E_gd,
                   lambda e, ttg=ttg, tpar=tpar: e.dma_start(
                       out=y_out[ttg * 128:(ttg + 1) * 128, :], in_=outb[tpar][:]), k=16)
                gd_out[ttg] = E_gd.n

        fin = E_gd.n
        op("gpsimd", [(E_gd, fin)], None, lambda e: e.nop())

        block = es.enter_context(nc.Block())

        @block.sync
        def _(sync):
            for th in P.q["sync"]:
                th(sync)

        @block.vector
        def _(vector):
            for th in P.q["vector"]:
                th(vector)

        @block.scalar
        def _(scalar):
            for th in P.q["scalar"]:
                th(scalar)

        @block.tensor
        def _(tensor):
            for th in P.q["tensor"]:
                th(tensor)

        @block.gpsimd
        def _(gpsimd):
            for th in P.q["gpsimd"]:
                th(gpsimd)

    return nc


def _prep_inputs(L0_attn_x, L0_attn_W_qkv, L0_ffn_w_gate, L0_ffn_w_up, L0_ffn_w_down):
    x = np.asarray(L0_attn_x, dtype=np.float32)
    wq = np.ascontiguousarray(np.asarray(L0_attn_W_qkv, dtype=np.float32).T)
    wg = np.ascontiguousarray(
        np.asarray(L0_ffn_w_gate, dtype=np.int32).T.reshape(D, NHC, 128).transpose(1, 0, 2))
    wu = np.ascontiguousarray(
        np.asarray(L0_ffn_w_up, dtype=np.int32).T.reshape(D, NHC, 128).transpose(1, 0, 2))
    wd = np.ascontiguousarray(np.asarray(L0_ffn_w_down, dtype=np.int32).T)
    ident = np.eye(128, dtype=ml_dtypes.bfloat16)
    b, s, d = x.shape
    half = s // 2
    shards = []
    for c in range(8):
        sh = np.ascontiguousarray(x[c // 2, (c % 2) * half:(c % 2 + 1) * half, :])
        shards.append({"x": sh, "wqkv": wq, "wg": wg, "wu": wu, "wd": wd, "ident": ident})
    return shards, (b, s, d)


def kernel(L0_attn_x, L0_attn_W_qkv, L0_ffn_w_gate, L0_ffn_w_up, L0_ffn_w_down):
    in_maps, (b, s, d) = _prep_inputs(L0_attn_x, L0_attn_W_qkv, L0_ffn_w_gate,
                                      L0_ffn_w_up, L0_ffn_w_down)
    nc = build_program(TCORE=s // 2)
    res = run_bass_kernel_spmd(nc, in_maps, core_ids=list(range(8)))
    out = np.empty((b, s, d), dtype=np.float32)
    half = s // 2
    for c in range(8):
        out[c // 2, (c % 2) * half:(c % 2 + 1) * half, :] = res.results[c]["y"]
    return out


# version 15
# speedup vs baseline: 1.0428x; 1.0342x over previous
import contextlib
import numpy as np
import ml_dtypes
import concourse.bass as bass
import concourse.mybir as mybir
from concourse.bass_utils import run_bass_kernel_spmd

F32 = mybir.dt.float32
BF16 = mybir.dt.bfloat16
I32 = mybir.dt.int32
AF = mybir.ActivationFunctionType
ALU = mybir.AluOpType
AX = mybir.AxisListType

D = 1024
H = 16
HD = 64
FF = 3072
NHC = 24
NDC = 8
EPS_LN = 1e-5
EPS_ATT = 1e-6
EPS_L2 = 1e-12
SCALE_ATT = float(64 ** -0.5)
DEQ = 1.0 / 127.0

ENGS = ("sync", "vector", "scalar", "tensor", "gpsimd")


class Edge:
    def __init__(self, es, nc, name):
        self.h = es.enter_context(nc.semaphore(name))
        self.n = 0


class Plan:
    def __init__(self):
        self.q = {e: [] for e in ENGS}
        self.lastw = {e: {} for e in ENGS}

    def op(self, eng, deps, inc, build, k=1):
        dd = []
        for ed, v in deps:
            if ed is None or v is None or v <= 0:
                continue
            key = id(ed)
            if self.lastw[eng].get(key, 0) >= v:
                continue
            self.lastw[eng][key] = v
            dd.append((ed.h, v))
        icap = None
        if inc is not None:
            inc.n += k
            icap = (inc.h, k)

        def thunk(eo, dd=tuple(dd), icap=icap, build=build):
            for h, v in dd:
                eo.wait_ge(h, v)
            ins = build(eo)
            if icap is not None:
                ins.then_inc(icap[0], icap[1])

        self.q[eng].append(thunk)
        return inc.n if inc is not None else None


def build_program(TCORE=4096, detect_races=False):
    NT = TCORE // 128
    TG = 512
    NTG = TCORE // TG
    NTT = TG // 128

    nc = bass.Bass("TRN2", target_bir_lowering=False, debug=False,
                   detect_race_conditions=detect_races)

    x_in = nc.dram_tensor("x", [TCORE, D], F32, kind="ExternalInput").ap()
    wqkv = nc.dram_tensor("wqkv", [D, 3 * D], F32, kind="ExternalInput").ap()
    wg_i = nc.dram_tensor("wg", [NHC, D, 128], I32, kind="ExternalInput").ap()
    wu_i = nc.dram_tensor("wu", [NHC, D, 128], I32, kind="ExternalInput").ap()
    wd_i = nc.dram_tensor("wd", [FF, D], I32, kind="ExternalInput").ap()
    ident_in = nc.dram_tensor("ident", [128, 128], BF16, kind="ExternalInput").ap()
    y_out = nc.dram_tensor("y", [TCORE, D], F32, kind="ExternalOutput").ap()

    wg_bf = nc.dram_tensor("wg_bf", [NHC, 128, 1024], BF16).ap()
    wu_bf = nc.dram_tensor("wu_bf", [NHC, 128, 1024], BF16).ap()
    qT_scr = nc.dram_tensor("qT_scr", [NT, 128, 1024], BF16).ap()
    xnT_scr = nc.dram_tensor("xnT_scr", [NT, 128, 1024], BF16).ap()
    cc_in = nc.dram_tensor("cc_in", [128, 520], F32)
    cc_out = nc.dram_tensor("cc_out", [128, 520], F32)

    es = contextlib.ExitStack()
    with es:
        def sb(name, shape, dt):
            return es.enter_context(nc.sbuf_tensor(name, shape, dt))

        xb4 = sb("xb4", [128, 4 * D], F32)
        xb = [xb4.ap()[:, i * D:(i + 1) * D] for i in range(4)]
        qn = sb("qn", [128, D], F32)
        kn = sb("kn", [128, D], F32)
        eab = sb("eab", [128, D], BF16)
        ep2b = sb("ep2b", [128, D], F32)
        outb = [sb("outb0", [128, D], F32), sb("outb1", [128, D], F32)]
        sq_k = sb("sq_k", [128, D], BF16)
        xnb = [sb("xnb0", [128, D], BF16), sb("xnb1", [128, D], BF16)]
        qhat = [sb("qhat0", [128, D], BF16), sb("qhat1", [128, D], BF16)]
        khat = [sb("khat0", [128, D], BF16), sb("khat1", [128, D], BF16)]
        vtile = [sb("vt0", [128, H * 65], BF16), sb("vt1", [128, H * 65], BF16)]
        xnT_t = [sb("xnTt0", [128, D], BF16), sb("xnTt1", [128, D], BF16),
                 sb("xnTt2", [128, D], BF16), sb("xnTt3", [128, D], BF16)]
        qh_st = [sb("qhst0", [128, D], BF16), sb("qhst1", [128, D], BF16)]
        wcast = [sb("wc0", [128, 1024], BF16), sb("wc1", [128, 1024], BF16)]
        xnT_g = [sb("xnTg0", [128, NDC * TG], BF16), sb("xnTg1", [128, NDC * TG], BF16)]
        _xbt = xb4.ap()[:, 2 * D:4 * D].bitcast(BF16)
        wg_sb = [_xbt[:, 0:1024], _xbt[:, 1024:2048]]
        wu_sb = [_xbt[:, 2048:3072], _xbt[:, 3072:4096]]
        wd_sb = sb("wd_sb", [128, NHC * 1024], BF16)
        bigW = sb("bigW", [128, NDC * 3 * D], BF16)
        kv_bd = sb("kv_bd", [128, 8 * 130], BF16)
        ident = sb("ident_sb", [128, 128], BF16)
        wi = [sb("wi0", [128, 1024], I32), sb("wi1", [128, 1024], I32)]
        qT_t = [wi[0].ap().bitcast(BF16)[:, 0:1024], wi[1].ap().bitcast(BF16)[:, 0:1024]]
        cst = sb("cst", [128, 3], F32)
        ssum = sb("ssum", [128, 4], F32)
        ssq = sb("ssq", [128, 4], F32)
        mb = sb("mb", [128, 4], F32)
        mm2 = sb("mm2", [128, 4], F32)
        varb = sb("varb", [128, 4], F32)
        stdv = sb("stdv", [128, 4], F32)
        rstd = sb("rstd", [128, 4], F32)
        negmr = sb("negmr", [128, 4], F32)
        ssq_q = sb("ssq_q", [128, H], F32)
        ssq_k = sb("ssq_k", [128, H], F32)
        nrm2_q = sb("nrm2_q", [128, H], F32)
        nrm2_k = sb("nrm2_k", [128, H], F32)
        rst_q = sb("rst_q", [128, H], F32)
        rst_k = sb("rst_k", [128, H], F32)
        rden = [sb("rden0", [128, H], F32), sb("rden1", [128, H], F32)]

        sqdump = outb[0].ap().bitcast(BF16)[:, 0:D]
        cc_sb = outb[1].ap()[:, 0:520]
        sg = [kn.ap()[:, 0:512], kn.ap()[:, 512:1024]]
        ep_scr1 = qn.ap()[:, 0:D]
        ep_scr2 = ep2b.ap()[:]
        P_rA = es.enter_context(nc.psum_tensor("P_rA", [128, 1024], F32))
        P_rB = es.enter_context(nc.psum_tensor("P_rB", [128, 1024], F32))
        P_kv = es.enter_context(nc.psum_tensor("P_kv", [128, 1024], F32))
        P_m1 = es.enter_context(nc.psum_tensor("P_m1", [128, 512], F32))
        P_m2 = es.enter_context(nc.psum_tensor("P_m2", [128, 512], F32))
        P_m1b = P_m1.ap().bitcast(BF16)
        P_m2b = P_m2.ap().bitcast(BF16)

        E_s = Edge(es, nc, "E_s")
        E_sd = Edge(es, nc, "E_sd")
        E_gd = Edge(es, nc, "E_gd")
        E_t = Edge(es, nc, "E_t")
        E_v = Edge(es, nc, "E_v")
        E_sc = Edge(es, nc, "E_sc")
        E_cc = Edge(es, nc, "E_cc")
        E_g = Edge(es, nc, "E_g")

        P = Plan()
        op = P.op

        s_x = {}; sc_sq = {}; v_var = {}; sc_sqrt = {}; v_stats = {}; sc_xn = {}
        t_trx = {}; v_evacX = {}; gd_xnt = {}
        t_q = {}; sc_qsq = {}; v_gq = {}; sc_nq = {}; v_qmul = {}; sc_expq = {}; sc_qhat = {}
        t_k = {}; sc_ksq = {}; v_gk = {}; sc_nk = {}; v_kmul = {}; sc_expk = {}; sc_khat = {}
        t_v = {}; v_vev = {}; t_kv = {}; t_trq = {}; v_evacQ = {}; gd_qt = {}
        gd_wg = {}; sc_cwg = {}; gd_swg = {}
        gd_wu = {}; v_cwu = {}; gd_swu = {}
        gd_wd = {}; sc_cwd = {}
        t_gate = {}; sc_silu = {}; t_up = {}; v_hmul = {}
        s_w = {}; s_xg = {}; s_x3 = {}; sd_qt = {}
        t_down = {}; t_attn = {}
        v_ep0 = {}; v_ep1 = {}; v_ep2 = {}; v_ep3 = {}; gd_out = {}

        op("sync", [], E_s, lambda e: e.dma_start(out=ident[:], in_=ident_in), k=16)
        s_ident = E_s.n
        op("vector", [], E_v, lambda e: e.memset(vtile[0][:], 1.0))
        op("vector", [], E_v, lambda e: e.memset(vtile[1][:], 1.0))
        op("vector", [], E_v, lambda e: e.memset(kv_bd[:], 0.0))
        op("vector", [], E_v, lambda e: e.memset(cst[:, 0:1], EPS_LN))
        op("vector", [], E_v, lambda e: e.memset(cst[:, 1:2], 1.0))
        op("vector", [], E_v, lambda e: e.memset(cst[:, 2:3], EPS_L2 * EPS_L2))
        v_init = E_v.n

        wq_cast = {}
        for kk in range(NDC * 3):
            dc, s = divmod(kk, 3)
            par = kk % 2
            deps = [(E_sd, None)]
            if kk >= 2:
                pe, pv = wq_cast[kk - 2]
                deps = [((E_sc if pe == "scalar" else E_v), pv)]
            op("scalar", deps, E_sd,
               lambda e, dc=dc, s=s, par=par: e.dma_start(
                   out=outb[par][:], in_=wqkv[dc * 128:(dc + 1) * 128, s * 1024:(s + 1) * 1024]), k=16)
            ld = E_sd.n
            eng = "scalar" if (kk % 2 == 0) else "vector"
            ed = E_sc if eng == "scalar" else E_v
            col = dc * 3 * D + s * 1024
            if eng == "scalar":
                op(eng, [(E_sd, ld)], ed,
                   lambda e, par=par, col=col: e.activation(
                       bigW[:, col:col + 1024], outb[par][:], AF.Copy))
            else:
                op(eng, [(E_sd, ld)], ed,
                   lambda e, par=par, col=col: e.tensor_copy(bigW[:, col:col + 1024], outb[par][:]))
            wq_cast[kk] = (eng, ed.n)
        sc_wq = E_sc.n
        v_wq = E_v.n

        def conv_sched(t):
            return range(t * NHC // NT, (t + 1) * NHC // NT)

        sc_qcopy = {}
        v_kcopy = {}
        v_xn = {}
        sc_evacQ = {}
        sc_qsoft = {}
        v_qhat = {}
        sc_khat = {}
        v_gq = {}
        v_gk = {}
        NP = NT // 2

        def front_x(t):
            op("sync", [(E_v, v_xn.get(t - 4)), (E_sc, sc_sq.get(t - 4))], E_s,
               lambda e, t=t: e.dma_start(out=xb[t % 4], in_=x_in[t * 128:(t + 1) * 128, :]), k=16)
            s_x[t] = E_s.n

        def round_mm(t, base, Pr, deps):
            tpar = t % 4
            for dc in range(NDC):
                for hf in range(2):
                    ed = E_t if (dc == NDC - 1 and hf == 1) else None
                    op("tensor", deps if (dc == 0 and hf == 0) else [], ed,
                       lambda e, dc=dc, hf=hf, tpar=tpar, Pr=Pr, base=base: e.matmul(
                           Pr.ap()[:, hf * 512:(hf + 1) * 512],
                           xnT_t[tpar][:, dc * 128:(dc + 1) * 128],
                           bigW[:, dc * 3 * D + base + hf * 512: dc * 3 * D + base + (hf + 1) * 512],
                           start=(dc == 0), stop=(dc == NDC - 1)))
            return E_t.n

        def kv_col(p):
            return (p // 4) * 512 + (p % 4) * 65

        def backT_kv(t):
            deps = [(E_sc, sc_khat[t]), (E_v, v_vev[t])]
            first = True
            for p in range(8):
                for eh in range(2):
                    h = 2 * p + eh
                    st = (t == 0) and (p % 4 == 0)
                    sp = (t == NT - 1) and (p % 4 == 3)
                    last = (p == 7) and (eh == 1)
                    op("tensor", deps if first else [], E_t if last else None,
                       lambda e, p=p, eh=eh, h=h, st=st, sp=sp, t=t: e.matmul(
                           P_kv.ap()[eh * 64:(eh + 1) * 64, kv_col(p):kv_col(p) + 65],
                           khat[t % 2][:, h * 64:(h + 1) * 64],
                           vtile[t % 2][:, h * 65:(h + 1) * 65],
                           start=st, stop=sp, skip_group_check=True))
                    first = False
            t_kv[t] = E_t.n

        cv_in = [wi[0].ap(), wi[1].ap(),
                 qn.ap().bitcast(I32), kn.ap().bitcast(I32)]
        cv_out = [wcast[0].ap()[:], wcast[1].ap()[:], qhat[0].ap()[:], khat[0].ap()[:]]
        cv_ld = {}
        cv_cast = {}
        cv_st = {}
        cv_tasks = []
        for hc in range(NHC):
            cv_tasks += [(0, hc), (1, hc), (2, hc)]

        def conv_task(k):
            kind, hc = cv_tasks[k]
            b = k % 4
            deps = []
            if k >= 4:
                pe, pv = cv_cast[k - 4]
                deps.append(((E_sc if pe == "scalar" else E_v), pv))
                if cv_st[k - 4] is not None:
                    deps.append((E_gd, cv_st[k - 4]))
            if kind == 0:
                op("gpsimd", deps, E_gd,
                   lambda e, hc=hc, b=b: e.dma_start(
                       out=cv_in[b].rearrange("p (dc c) -> p dc c", dc=NDC),
                       in_=wg_i[hc].rearrange("(dc p) c -> p dc c", p=128)), k=16)
            elif kind == 1:
                op("gpsimd", deps, E_gd,
                   lambda e, hc=hc, b=b: e.dma_start(
                       out=cv_in[b].rearrange("p (dc c) -> p dc c", dc=NDC),
                       in_=wu_i[hc].rearrange("(dc p) c -> p dc c", p=128)), k=16)
            else:
                op("gpsimd", deps, E_gd,
                   lambda e, hc=hc, b=b: e.dma_start(
                       out=cv_in[b], in_=wd_i[hc * 128:(hc + 1) * 128, :]), k=16)
            cv_ld[k] = E_gd.n
            eng = "scalar" if (k % 2 == 0) else "vector"
            ed = E_sc if eng == "scalar" else E_v
            if kind == 2:
                dst = wd_sb.ap()[:, hc * 1024:(hc + 1) * 1024]
            else:
                dst = cv_out[b]
            if eng == "scalar":
                op(eng, [(E_gd, cv_ld[k])], ed,
                   lambda e, b=b, dst=dst: e.activation(dst, cv_in[b], AF.Copy, scale=DEQ))
            else:
                op(eng, [(E_gd, cv_ld[k])], ed,
                   lambda e, b=b, dst=dst: e.tensor_scalar_mul(dst, cv_in[b], DEQ))
            cv_cast[k] = (eng, ed.n)
            if kind == 2:
                cv_st[k] = None
                sc_cwd[hc] = ed.n if eng == "scalar" else None
                g_conv[hc] = (E_sc if eng == "scalar" else E_v, ed.n)
            else:
                dstd = wg_bf[hc] if kind == 0 else wu_bf[hc]
                op("gpsimd", [((E_sc if eng == "scalar" else E_v), ed.n)], E_gd,
                   lambda e, dstd=dstd, b=b, dst=dst: e.dma_start(out=dstd, in_=dst), k=16)
                cv_st[k] = E_gd.n
                (gd_swg if kind == 0 else gd_swu)[hc] = E_gd.n

        def conv(tp):
            k0 = tp * len(cv_tasks) // NP
            k1 = (tp + 1) * len(cv_tasks) // NP
            for k in range(k0, k1):
                conv_task(k)

        g_conv = {}

        def ladder(t, which):
            par = t % 2
            Pr = ((P_rA if t % 2 == 0 else P_rB) if which == 0
                  else (P_rB if t % 2 == 0 else P_rA))
            if which == 0:
                op("scalar", [(E_t, t_q[t]), (E_v, v_qhat.get(t - 1))], E_sc,
                   lambda e, Pr=Pr: e.copy(qn[:], Pr.ap()[:]))
                sc_qcopy[t] = E_sc.n
                op("scalar", [(E_sc, E_sc.n), (E_v, v_gq.get(t - 1))], E_sc,
                   lambda e: e.activation(eab[:], qn[:], AF.Square))
                op("vector", [(E_sc, E_sc.n)], E_v,
                   lambda e: e.reduce_sum(
                       ssq_q[:], eab[:].rearrange("p (h d) -> p h d", h=H), axis=AX.X))
                v_gq[t] = E_v.n
                op("scalar", [(E_v, E_v.n)], E_sc,
                   lambda e: e.activation(nrm2_q[:], ssq_q[:], AF.Ln, bias=cst[:, 2:3]))
                op("scalar", [(E_sc, E_sc.n)], E_sc,
                   lambda e: e.activation(rst_q[:], nrm2_q[:], AF.Exp, scale=-0.5))
                qm_deps = [(E_sc, E_sc.n), (E_t, t_trq.get(t - 2))]
                if t == 0:
                    qm_deps.append(cv_out_guard.get(2))
                op("vector", [d for d in qm_deps if d], E_v,
                   lambda e, par=par: e.tensor_tensor(
                       qhat[par][:].rearrange("p (h d) -> p h d", h=H),
                       qn[:].rearrange("p (h d) -> p h d", h=H),
                       rst_q[:].unsqueeze(2).broadcast_to([128, H, HD]),
                       ALU.mult))
                v_qhat[t] = E_v.n
            else:
                op("vector", [(E_t, t_k[t]), (E_v, v_kmul.get(t - 1))], E_v,
                   lambda e, Pr=Pr: e.tensor_copy(kn[:], Pr.ap()[:]))
                v_kcopy[t] = E_v.n
                op("scalar", [(E_v, E_v.n), (E_v, v_gk.get(t - 1))], E_sc,
                   lambda e: e.activation(sq_k[:], kn[:], AF.Square))
                op("vector", [(E_sc, E_sc.n)], E_v,
                   lambda e: e.reduce_sum(
                       ssq_k[:], sq_k[:].rearrange("p (h d) -> p h d", h=H), axis=AX.X))
                v_gk[t] = E_v.n
                op("scalar", [(E_v, E_v.n)], E_sc,
                   lambda e: e.activation(nrm2_k[:], ssq_k[:], AF.Ln, bias=cst[:, 2:3]))
                op("scalar", [(E_sc, E_sc.n)], E_sc,
                   lambda e: e.activation(rst_k[:], nrm2_k[:], AF.Exp, scale=-0.5))
                op("vector", [(E_sc, E_sc.n)], E_v,
                   lambda e: e.tensor_tensor(
                       eab[:].rearrange("p (h d) -> p h d", h=H),
                       kn[:].rearrange("p (h d) -> p h d", h=H),
                       rst_k[:].unsqueeze(2).broadcast_to([128, H, HD]),
                       ALU.mult))
                v_kmul[t] = E_v.n
                op("scalar", [(E_v, E_v.n)], E_sc,
                   lambda e: e.activation(sq_k[:], eab[:], AF.Exp))
                lk_deps = [(E_sc, E_sc.n), (E_t, t_kv.get(t - 2))]
                if t == 0:
                    lk_deps.append(cv_out_guard.get(3))
                op("scalar", [d for d in lk_deps if d], E_sc,
                   lambda e, par=par: e.activation(khat[par][:], sq_k[:], AF.Ln, bias=cst[:, 1:2]))
                sc_khat[t] = E_sc.n

        def back_tile(t):
            t_q[t] = round_mm(t, 0, P_rA if t % 2 == 0 else P_rB,
                              [(E_s, s_xnl[t]), (E_v, v_kcopy.get(t - 1)),
                               (E_sc, sc_wq), (E_v, v_wq)])
            ladder(t, 0)
            t_k[t] = round_mm(t, D, P_rB if t % 2 == 0 else P_rA,
                              [(E_v, v_vev.get(t - 1))])
            ladder(t, 1)
            t_v[t] = round_mm(t, 2 * D, P_rA if t % 2 == 0 else P_rB,
                              [(E_sc, sc_qcopy[t])])
            op("vector", [(E_t, t_v[t]), (E_t, t_kv.get(t - 2)), (E_v, v_init)], E_v,
               lambda e, t=t: e.tensor_copy(
                   vtile[t % 2][:].rearrange("p (h c) -> p h c", h=H)[:, :, 0:64],
                   (P_rA if t % 2 == 0 else P_rB).ap().rearrange("p (h d) -> p h d", h=H)))
            v_vev[t] = E_v.n

        def trq_block(t):
            Pm = P_m1b if t % 2 == 0 else P_m2b
            depsq = [(E_v, v_qhat[t]), (E_sc, sc_evacQ.get(t - 2))]
            if t < 2:
                depsq.append((E_v, v_evacX.get(NT - 1)))
            for dc in range(NDC):
                op("tensor", depsq if dc == 0 else [],
                   E_t if dc == NDC - 1 else None,
                   lambda e, dc=dc, t=t, Pm=Pm: e.transpose(
                       Pm[:, dc * 128:(dc + 1) * 128],
                       qhat[t % 2][:, dc * 128:(dc + 1) * 128], ident[:]))
            t_trq[t] = E_t.n
            op("scalar", [(E_t, E_t.n), (E_gd, gd_qt.get(t - 2))], E_sc,
               lambda e, t=t, Pm=Pm: e.copy(qh_st[t % 2][:], Pm[:, 0:1024]))
            sc_evacQ[t] = E_sc.n
            op("gpsimd", [(E_sc, E_sc.n)], E_gd,
               lambda e, t=t: e.dma_start(out=qT_scr[t], in_=qh_st[t % 2][:]), k=16)
            gd_qt[t] = E_gd.n

        def front_tile(t, i, fp):
            sl = 2 * (fp % 2)
            op("vector", [(E_sc, sc_sqrt[fp]), (E_v, v_stats[fp]),
                          (E_t, t_trx.get(t - 2))], E_v,
               lambda e, i=i, t=t, sl=sl: e.tensor_scalar(
                   xnb[t % 2][:], xb[t % 4], negmr[:, sl + i:sl + i + 1], rstd[:, sl + i:sl + i + 1],
                   ALU.add, ALU.mult))
            v_xn[t] = E_v.n
            Pm = P_m1b if t % 2 == 0 else P_m2b
            depsx = [(E_v, v_xn[t]), (E_v, v_evacX.get(t - 2)), (E_s, s_ident)]
            for dc in range(NDC):
                op("tensor", depsx if dc == 0 else [],
                   E_t if dc == NDC - 1 else None,
                   lambda e, dc=dc, t=t, Pm=Pm: e.transpose(
                       Pm[:, dc * 128:(dc + 1) * 128],
                       xnb[t % 2][:, dc * 128:(dc + 1) * 128], ident[:]))
            t_trx[t] = E_t.n
            op("vector", [(E_t, E_t.n), (E_gd, gd_xnt.get(t - 4))], E_v,
               lambda e, t=t, Pm=Pm: e.tensor_copy(xnT_t[t % 4][:], Pm[:, 0:1024]))
            v_evacX[t] = E_v.n
            op("gpsimd", [(E_v, v_evacX[t])], E_gd,
               lambda e, t=t: e.dma_start(out=xnT_scr[t], in_=xnT_t[t % 4][:]), k=16)
            gd_xnt[t] = E_gd.n

        for fp in range(NP):
            front_x(2 * fp)
            front_x(2 * fp + 1)
            if True:
                a, b = 2 * fp, 2 * fp + 1
                sl = 2 * (fp % 2)
                op("scalar", [(E_s, s_x[a]), (E_v, v_stats.get(fp - 2)), (E_v, v_var.get(fp - 2))], E_sc,
                   lambda e, a=a, sl=sl: e.activation(sqdump, xb[a % 4], AF.Square, accum_out=ssq[:, sl:sl + 1]))
                op("scalar", [(E_s, s_x[b])], E_sc,
                   lambda e, b=b, sl=sl: e.activation(sqdump, xb[b % 4], AF.Square, accum_out=ssq[:, sl + 1:sl + 2]))
                sc_sq[a] = sc_sq[b] = E_sc.n
                op("vector", [(E_s, s_x[b]), (E_v, v_stats.get(fp - 2))], E_v,
                   lambda e, a=a, sl=sl: e.reduce_sum(
                       ssum[:, sl:sl + 2], xb4.ap()[:, (a % 4) * D:(a % 4 + 2) * D]
                       .rearrange("p (i d) -> p i d", i=2), axis=AX.X))
                op("vector", [(E_v, E_v.n)], E_v,
                   lambda e, sl=sl: e.tensor_scalar_mul(mb[:, sl:sl + 2], ssum[:, sl:sl + 2], 1.0 / D))
                op("vector", [(E_v, E_v.n)], E_v,
                   lambda e, sl=sl: e.tensor_mul(mm2[:, sl:sl + 2], mb[:, sl:sl + 2], mb[:, sl:sl + 2]))
                op("vector", [(E_v, E_v.n), (E_sc, sc_sq[b])], E_v,
                   lambda e, sl=sl: e.tensor_scalar(varb[:, sl:sl + 2], ssq[:, sl:sl + 2], 1.0 / D, EPS_LN, ALU.mult, ALU.add))
                op("vector", [(E_v, E_v.n)], E_v,
                   lambda e, sl=sl: e.tensor_sub(varb[:, sl:sl + 2], varb[:, sl:sl + 2], mm2[:, sl:sl + 2]))
                v_var[fp] = E_v.n
                op("scalar", [(E_v, E_v.n)], E_sc,
                   lambda e, sl=sl: e.activation(stdv[:, sl:sl + 2], varb[:, sl:sl + 2], AF.Ln))
                op("scalar", [(E_sc, E_sc.n)], E_sc,
                   lambda e, sl=sl: e.activation(rstd[:, sl:sl + 2], stdv[:, sl:sl + 2], AF.Exp, scale=-0.5))
                sc_sqrt[fp] = E_sc.n
                op("vector", [(E_v, v_var[fp])], E_v,
                   lambda e, sl=sl: e.tensor_scalar_mul(negmr[:, sl:sl + 2], mb[:, sl:sl + 2], -1.0))
                v_stats[fp] = E_v.n
                front_tile(a, 0, fp)
                front_tile(b, 1, fp)
            conv(fp)

        cv_stage_guard = {}
        cv_out_guard = {}
        for k in range(len(cv_tasks)):
            b = k % 4
            pe, pv = cv_cast[k]
            cv_stage_guard[b] = ((E_sc if pe == "scalar" else E_v), pv)
            if cv_st.get(k) is not None:
                cv_out_guard[b] = (E_gd, cv_st[k])
        wd_guards = []
        _smax = _vmax = 0
        for k in range(len(cv_tasks)):
            if cv_tasks[k][0] == 2:
                pe, pv = cv_cast[k]
                if pe == "scalar":
                    _smax = max(_smax, pv)
                else:
                    _vmax = max(_vmax, pv)
        if _smax:
            wd_guards.append((E_sc, _smax))
        if _vmax:
            wd_guards.append((E_v, _vmax))

        s_xnl = {}

        def xnl(t):
            op("sync", [(E_t, t_v.get(t - 4)), (E_gd, gd_xnt[t])], E_s,
               lambda e, t=t: e.dma_start(out=xnT_t[t % 4][:], in_=xnT_scr[t]), k=16)
            s_xnl[t] = E_s.n

        for ip in range(NP + 2):
            lp = ip if ip < NP else None
            bp = ip - 1 if 1 <= ip <= NP else None
            kp = ip - 2 if ip >= 2 else None
            if lp is not None:
                xnl(2 * lp)
                xnl(2 * lp + 1)
            if kp is not None:
                trq_block(2 * kp)
                trq_block(2 * kp + 1)
                backT_kv(2 * kp)
                backT_kv(2 * kp + 1)
            if bp is not None:
                back_tile(2 * bp)
                back_tile(2 * bp + 1)
        op("vector", [(E_t, t_kv[NT - 1])], E_v,
           lambda e: e.tensor_copy(
               cc_sb.rearrange("p (b q c) -> p b q c", b=2, q=4),
               P_kv.ap().rearrange("p (b f) -> p b f", b=2)[:, :, 0:260]
               .rearrange("p b (q c) -> p b q c", q=4)))
        v_ccevac = E_v.n
        op("gpsimd", [(E_v, E_v.n)], E_gd,
           lambda e: e.dma_start(out=cc_in.ap(), in_=cc_sb), k=16)
        op("gpsimd", [(E_gd, E_gd.n)], E_cc,
           lambda e: e.collective_compute(
               "AllReduce", ALU.add,
               replica_groups=[[0, 1], [2, 3], [4, 5], [6, 7]],
               ins=[cc_in.ap().opt()], outs=[cc_out.ap().opt()]))
        op("gpsimd", [(E_cc, 1)], E_gd,
           lambda e: e.dma_start(out=cc_sb, in_=cc_out.ap()), k=16)
        gd_ccout = E_gd.n

        ccv = cc_sb.rearrange("p (b q c) -> p b q c", b=2, q=4)

        def build_kvbd():
            deps = [(E_gd, gd_ccout), (E_v, v_init)]
            first = True
            for p in range(8):
                for eh in range(2):
                    op("vector", deps if first else [], E_v,
                       lambda e, p=p, eh=eh: e.tensor_scalar_mul(
                           kv_bd[eh * 64:(eh + 1) * 64, p * 130 + eh * 64: p * 130 + eh * 64 + 64],
                           ccv[eh * 64:(eh + 1) * 64, p // 4, p % 4, 0:64], SCALE_ATT))
                    first = False
                    op("vector", [], E_v,
                       lambda e, p=p, eh=eh: e.tensor_scalar_add(
                           kv_bd[eh * 64:(eh + 1) * 64, p * 130 + 128 + eh: p * 130 + 129 + eh],
                           ccv[eh * 64:(eh + 1) * 64, p // 4, p % 4, 64:65], EPS_ATT))
            return E_v.n

        v_kvbd = None
        wg_rd = [None, None]
        wu_rd = [None, None]
        xg_rd = [None, None]
        sg_rd = [None, None]
        silu_last = None
        hmul_last = None
        down_last = None
        for g in range(NTG):
            gpar = g % 2
            depsg = [(E_gd, gd_xnt[g * NTT + NTT - 1]), (E_t, xg_rd[gpar])]
            first = True
            for tb in range(NTT):
                op("sync", depsg if first else [], E_s,
                   lambda e, tb=tb, g=g, gpar=gpar: e.dma_start(
                       out=xnT_g[gpar][:].rearrange("p (dc tb c) -> p dc tb c", dc=NDC, tb=NTT)
                       [:, :, tb, :],
                       in_=xnT_scr[g * NTT + tb].rearrange("p (dc c) -> p dc c", dc=NDC)), k=16)
                first = False
            s_xg[g] = E_s.n
            for hc in range(NHC):
                hpar = hc % 2
                wgl_deps = [(E_t, wg_rd[hpar]), (E_gd, gd_swg.get(NHC - 1))]
                if g == 0 and hc < 2:
                    wgl_deps += [(E_v, v_xn.get(NT - 1)), (E_sc, sc_sq.get(NT - 1))]
                op("sync", wgl_deps, E_s,
                   lambda e, hc=hc, hpar=hpar: e.dma_start(out=wg_sb[hpar], in_=wg_bf[hc]), k=16)
                op("sync", [(E_t, wu_rd[hpar]), (E_gd, gd_swu.get(NHC - 1))], E_s,
                   lambda e, hc=hc, hpar=hpar: e.dma_start(out=wu_sb[hpar], in_=wu_bf[hc]), k=16)
                s_w[(g, hc)] = E_s.n
                depsgate = [(E_s, s_w[(g, hc)]), (E_s, s_xg[g]),
                            (E_sc, silu_last),
                            (E_sc, sc_evacQ.get(NT - 2))] + wd_guards
                for dc in range(NDC):
                    op("tensor", depsgate if dc == 0 else [],
                       E_t if dc == NDC - 1 else None,
                       lambda e, dc=dc, hpar=hpar, gpar=gpar: e.matmul(
                           P_m1.ap()[:],
                           wg_sb[hpar][:, dc * 128:(dc + 1) * 128],
                           xnT_g[gpar][:, dc * TG:(dc + 1) * TG],
                           start=(dc == 0), stop=(dc == NDC - 1)))
                t_gate[(g, hc)] = E_t.n
                wg_rd[hpar] = E_t.n
                sdeps = [(E_t, E_t.n), (E_v, sg_rd[hpar])]
                if g == 0 and hc < 2:
                    sdeps.append((E_v, v_kmul.get(NT - 1)))
                op("scalar", sdeps, E_sc,
                   lambda e, hpar=hpar: e.activation(sg[hpar], P_m1.ap()[:], AF.Silu))
                sc_silu[(g, hc)] = E_sc.n
                silu_last = E_sc.n
                depsup = [(E_v, hmul_last)]
                if g == 0 and hc == 0:
                    depsup.append((E_sc, sc_evacQ.get(NT - 1)))
                for dc in range(NDC):
                    op("tensor", depsup if dc == 0 else [],
                       E_t if dc == NDC - 1 else None,
                       lambda e, dc=dc, hpar=hpar, gpar=gpar: e.matmul(
                           P_m2.ap()[:],
                           wu_sb[hpar][:, dc * 128:(dc + 1) * 128],
                           xnT_g[gpar][:, dc * TG:(dc + 1) * TG],
                           start=(dc == 0), stop=(dc == NDC - 1)))
                t_up[(g, hc)] = E_t.n
                wu_rd[hpar] = E_t.n
                xg_rd[gpar] = E_t.n
                depsh = [(E_t, E_t.n), (E_sc, sc_silu[(g, hc)]), (E_t, down_last)]
                if g == 0 and hc == 0:
                    depsh.append((E_t, t_v.get(NT - 1)))
                op("vector", depsh, E_v,
                   lambda e, hc=hc, hpar=hpar: e.tensor_tensor(
                       bigW[:, hc * TG:(hc + 1) * TG], P_m2.ap()[:], sg[hpar], ALU.mult))
                v_hmul[(g, hc)] = E_v.n
                sg_rd[hpar] = E_v.n
                hmul_last = E_v.n
            if g == 0:
                v_kvbd = build_kvbd()
            for tt in range(NTT):
                ttg = g * NTT + tt
                tpar = tt % 2
                P_dn = P_kv if tpar == 0 else P_rB
                P_de = P_rB if tpar == 0 else P_kv
                xdeps = [(E_v, v_ep2.get(ttg - 2))]
                if ttg < 2:
                    xdeps.append((E_v, v_xn.get(NT - 1)))
                    xdeps.append((E_sc, sc_sq.get(NT - 1)))
                op("sync", xdeps, E_s,
                   lambda e, ttg=ttg, tpar=tpar: e.dma_start(
                       out=xb[tpar], in_=x_in[ttg * 128:(ttg + 1) * 128, :]), k=16)
                s_x3[ttg] = E_s.n
                qtdeps = [(E_t, t_attn.get(ttg - 2)), (E_gd, gd_qt.get(ttg))]
                if ttg < 2:
                    qtdeps += [cv_stage_guard.get(0), cv_stage_guard.get(1)]
                qtdeps = [d for d in qtdeps if d]
                op("scalar", qtdeps, E_sd,
                   lambda e, ttg=ttg, tpar=tpar: e.dma_start(out=qT_t[tpar], in_=qT_scr[ttg]), k=16)
                sd_qt[ttg] = E_sd.n
                sfdeps = [(E_sd, E_sd.n)]
                if ttg == 0:
                    sfdeps.append((E_v, v_kmul.get(NT // 2 - 1)))
                op("scalar", sfdeps, E_sc,
                   lambda e, tpar=tpar: e.activation(eab[:, 0:D], qT_t[tpar], AF.Exp))
                op("scalar", [(E_sc, E_sc.n)], E_sc,
                   lambda e, tpar=tpar: e.activation(qT_t[tpar], eab[:, 0:D], AF.Ln, bias=cst[:, 1:2]))
                sc_qsoft[ttg] = E_sc.n
                depsd = [(E_v, v_hmul[(g, NHC - 1)]), (E_v, v_ep3.get(ttg - 2)),
                         (E_v, v_ep0.get(ttg - 1))]
                if ttg == 0:
                    depsd.append((E_v, v_ccevac))
                if ttg == 1:
                    depsd.append((E_v, v_vev.get(NT - 1)))
                first = True
                for hc in range(NHC):
                    for hf in range(2):
                        last = (hc == NHC - 1 and hf == 1)
                        op("tensor", depsd if first else [], E_t if last else None,
                           lambda e, hc=hc, hf=hf, tt=tt, P_dn=P_dn: e.matmul(
                               P_dn.ap()[:, hf * 512:(hf + 1) * 512],
                               bigW[:, hc * TG + tt * 128: hc * TG + (tt + 1) * 128],
                               wd_sb[:, hc * 1024 + hf * 512: hc * 1024 + (hf + 1) * 512],
                               start=(hc == 0), stop=(hc == NHC - 1)))
                        first = False
                t_down[ttg] = E_t.n
                down_last = E_t.n
                depsa = [(E_v, v_kvbd), (E_sc, sc_qsoft[ttg]),
                         (E_v, v_ep1.get(ttg - 1)), (E_v, v_ep3.get(ttg - 1))]
                if ttg == 0:
                    depsa += [(E_v, v_vev.get(NT - 1)), (E_v, v_kcopy.get(NT - 1))]
                first = True
                for p in range(8):
                    op("tensor", depsa if first else [], None,
                       lambda e, p=p, tpar=tpar: e.matmul(
                           P_rA.ap()[:, p * 128:(p + 1) * 128],
                           qT_t[tpar][:, p * 128:(p + 1) * 128],
                           kv_bd[:, p * 130: p * 130 + 128],
                           start=(p % 4 == 0), stop=(p % 4 == 3)))
                    first = False
                    op("tensor", [], E_t if p == 7 else None,
                       lambda e, p=p, tpar=tpar, P_de=P_de: e.matmul(
                           P_de.ap()[:, 2 * p: 2 * p + 2],
                           qT_t[tpar][:, p * 128:(p + 1) * 128],
                           kv_bd[:, p * 130 + 128: p * 130 + 130],
                           start=(p == 0), stop=(p == 7)))
                t_attn[ttg] = E_t.n
                epdeps = [(E_t, E_t.n), (E_v, v_ep1.get(ttg - 2))]
                if ttg == 0:
                    epdeps += [(E_v, v_qhat.get(NT - 1)), (E_v, v_kmul.get(NT - 1)),
                               (E_sc, sc_sq.get(NT - 1)), (E_sc, sc_wq), (E_v, v_wq)]
                op("vector", epdeps, E_v,
                   lambda e, tpar=tpar, P_de=P_de: e.reciprocal(rden[tpar][:], P_de.ap()[:, 0:16]))
                v_ep0[ttg] = E_v.n
                op("vector", [(E_v, E_v.n)], E_v,
                   lambda e, tpar=tpar: e.tensor_tensor(
                       ep_scr1.rearrange("p (a b c) -> p a b c", a=8, b=2),
                       P_rA.ap().rearrange("p (a b c) -> p a b c", a=8, b=2),
                       rden[tpar][:].rearrange("p (a b) -> p a b", a=8)
                       .unsqueeze(3).broadcast_to([128, 8, 2, 64]),
                       ALU.mult))
                v_ep1[ttg] = E_v.n
                op("vector", [(E_v, E_v.n), (E_s, s_x3[ttg])], E_v,
                   lambda e, tpar=tpar: e.tensor_add(ep_scr2, ep_scr1, xb[tpar]))
                v_ep2[ttg] = E_v.n
                ep3deps = [(E_v, E_v.n), (E_t, t_down[ttg]), (E_gd, gd_out.get(ttg - 2))]
                if ttg == 1:
                    ep3deps.append((E_v, v_kvbd))
                op("vector", ep3deps, E_v,
                   lambda e, tpar=tpar, P_dn=P_dn: e.tensor_add(outb[tpar][:], ep_scr2, P_dn.ap()[:]))
                v_ep3[ttg] = E_v.n
                op("gpsimd", [(E_v, E_v.n)], E_gd,
                   lambda e, ttg=ttg, tpar=tpar: e.dma_start(
                       out=y_out[ttg * 128:(ttg + 1) * 128, :], in_=outb[tpar][:]), k=16)
                gd_out[ttg] = E_gd.n

        fin = E_gd.n
        op("gpsimd", [(E_gd, fin)], None, lambda e: e.nop())

        block = es.enter_context(nc.Block())

        @block.sync
        def _(sync):
            for th in P.q["sync"]:
                th(sync)

        @block.vector
        def _(vector):
            for th in P.q["vector"]:
                th(vector)

        @block.scalar
        def _(scalar):
            for th in P.q["scalar"]:
                th(scalar)

        @block.tensor
        def _(tensor):
            for th in P.q["tensor"]:
                th(tensor)

        @block.gpsimd
        def _(gpsimd):
            for th in P.q["gpsimd"]:
                th(gpsimd)

    return nc


def _prep_inputs(L0_attn_x, L0_attn_W_qkv, L0_ffn_w_gate, L0_ffn_w_up, L0_ffn_w_down):
    x = np.asarray(L0_attn_x, dtype=np.float32)
    wq = np.ascontiguousarray(np.asarray(L0_attn_W_qkv, dtype=np.float32).T)
    wg = np.ascontiguousarray(
        np.asarray(L0_ffn_w_gate, dtype=np.int32).T.reshape(D, NHC, 128).transpose(1, 0, 2))
    wu = np.ascontiguousarray(
        np.asarray(L0_ffn_w_up, dtype=np.int32).T.reshape(D, NHC, 128).transpose(1, 0, 2))
    wd = np.ascontiguousarray(np.asarray(L0_ffn_w_down, dtype=np.int32).T)
    ident = np.eye(128, dtype=ml_dtypes.bfloat16)
    b, s, d = x.shape
    half = s // 2
    shards = []
    for c in range(8):
        sh = np.ascontiguousarray(x[c // 2, (c % 2) * half:(c % 2 + 1) * half, :])
        shards.append({"x": sh, "wqkv": wq, "wg": wg, "wu": wu, "wd": wd, "ident": ident})
    return shards, (b, s, d)


def kernel(L0_attn_x, L0_attn_W_qkv, L0_ffn_w_gate, L0_ffn_w_up, L0_ffn_w_down):
    in_maps, (b, s, d) = _prep_inputs(L0_attn_x, L0_attn_W_qkv, L0_ffn_w_gate,
                                      L0_ffn_w_up, L0_ffn_w_down)
    nc = build_program(TCORE=s // 2)
    res = run_bass_kernel_spmd(nc, in_maps, core_ids=list(range(8)))
    out = np.empty((b, s, d), dtype=np.float32)
    half = s // 2
    for c in range(8):
        out[c // 2, (c % 2) * half:(c % 2 + 1) * half, :] = res.results[c]["y"]
    return out
